# version 12
# baseline (speedup 1.0000x reference)
import numpy as np
from contextlib import ExitStack

import concourse.bass as bass
import concourse.mybir as mybir
from concourse.bass_utils import run_bass_kernel_spmd

F32 = mybir.dt.float32
BF16 = mybir.dt.bfloat16
AF = mybir.ActivationFunctionType
ALU = mybir.AluOpType
AX = mybir.AxisListType

B = 8
T = 2048
D = 1024
DI = 2048
NST = 16
RK = 64
NDB = DI // 128
NMB = D // 128
NE = 8
FF = 2048
NFB = FF // 128
EPS = 1e-5
TQ = 512
NQ = T // TQ
NTB = TQ // 128

ALL = ("*ALL*",)
DBG = None


class Sched:

    ENGINES = ("tensor", "vector", "scalar", "gpsimd", "sync")

    def __init__(self):
        self.ops = []
        self.regions = {}

    def add(self, engine, emit, reads=(), writes=(), dma=False, tag=""):
        idx = len(self.ops)
        op = dict(i=idx, eng=engine, emit=emit, dma=dma, tag=tag, deps=set())
        reads = list(reads)
        writes = list(writes)

        def conflicts(name, sub):
            d = self.regions.get(name)
            if d is None:
                return []
            if sub is ALL:
                return list(d.keys())
            out = []
            if sub in d:
                out.append(sub)
            if ALL in d:
                out.append(ALL)
            return out

        for (name, sub) in reads:
            for k in conflicts(name, sub):
                w, _ = self.regions[name][k]
                if w is not None:
                    op["deps"].add(w["i"])
        for (name, sub) in writes:
            for k in conflicts(name, sub):
                w, rs = self.regions[name][k]
                if w is not None:
                    op["deps"].add(w["i"])
                for r in rs:
                    op["deps"].add(r["i"])
        for (name, sub) in reads:
            d = self.regions.setdefault(name, {})
            if sub not in d:
                d[sub] = (None, [])
            d[sub][1].append(op)
        for (name, sub) in writes:
            d = self.regions.setdefault(name, {})
            if sub is ALL:
                for k in list(d.keys()):
                    d[k] = (op, [])
                d[ALL] = (op, [])
            else:
                d[sub] = (op, [])
        self.ops.append(op)
        return op

    def finalize(self, block, sems, dma_sems):
        ops = self.ops
        pos = {e: 0 for e in self.ENGINES}
        for op in ops:
            op["pos"] = pos[op["eng"]]
            pos[op["eng"]] += 1
        for op in ops:
            op["deps"] = {d for d in op["deps"]
                          if ops[d]["dma"] or ops[d]["eng"] != op["eng"]
                          or (op["eng"] != "tensor"
                              and (op["pos"] - ops[d]["pos"]) <= 6)}
        waited = set()
        for op in ops:
            waited.update(op["deps"])
        cnt = {e: 0 for e in self.ENGINES}
        dcnt = {e: 0 for e in self.ENGINES}
        for op in ops:
            if op["dma"]:
                dcnt[op["eng"]] += 1
                op["count"] = dcnt[op["eng"]] * 16
                op["inc"] = True
            else:
                op["inc"] = op["i"] in waited
                if op["inc"]:
                    cnt[op["eng"]] += 1
                op["count"] = cnt[op["eng"]]

        def make_body(eng_name):
            mine = [op for op in ops if op["eng"] == eng_name]

            def body(eng):
                seen = {}
                for op in mine:
                    needs = {}
                    for d in op["deps"]:
                        dop = ops[d]
                        key = (dop["eng"], dop["dma"])
                        if needs.get(key, 0) < dop["count"]:
                            needs[key] = dop["count"]
                    for key in sorted(needs, key=lambda k: (k[0], k[1])):
                        v = needs[key]
                        if seen.get(key, 0) >= v:
                            continue
                        tgt, is_dma = key
                        eng.wait_ge(dma_sems[tgt] if is_dma else sems[tgt], v)
                        seen[key] = v
                    inst = op["emit"](eng)
                    if op["inc"]:
                        assert inst is not None, f"op {op['tag']} needs inst"
                        if op["dma"]:
                            inst.then_inc(dma_sems[eng_name], 16)
                        else:
                            inst.then_inc(sems[eng_name], 1)
            return body

        for name in self.ENGINES:
            getattr(block, name)(make_body(name))


def build_program():
    nc = bass.Bass("TRN2", target_bir_lowering=False)

    h_x = nc.dram_tensor("x", [T, D], F32, kind="ExternalInput").ap()
    h_win = nc.dram_tensor("w_in", [D, 2 * DI], F32, kind="ExternalInput").ap()
    h_convw = nc.dram_tensor("conv_w", [DI, 1, 4], F32, kind="ExternalInput").ap()
    h_convb = nc.dram_tensor("conv_b", [DI], F32, kind="ExternalInput").ap()
    h_xpw = nc.dram_tensor("x_proj_w", [DI, RK + 2 * NST], F32,
                           kind="ExternalInput").ap()
    h_dtw = nc.dram_tensor("dt_w", [RK, DI], F32, kind="ExternalInput").ap()
    h_dtb = nc.dram_tensor("dt_b", [DI], F32, kind="ExternalInput").ap()
    h_alog = nc.dram_tensor("a_log", [DI, NST], F32, kind="ExternalInput").ap()
    h_dd = nc.dram_tensor("d_vec", [DI], F32, kind="ExternalInput").ap()
    h_wout = nc.dram_tensor("w_out", [DI, D], F32, kind="ExternalInput").ap()
    h_nw = nc.dram_tensor("norm_w", [D], F32, kind="ExternalInput").ap()
    h_wg = nc.dram_tensor("wg", [D, NE], F32, kind="ExternalInput").ap()
    h_wu = nc.dram_tensor("wu", [NE, D, FF], F32, kind="ExternalInput").ap()
    h_wd = nc.dram_tensor("wd", [NE, FF, D], F32, kind="ExternalInput").ap()
    h_w1 = nc.dram_tensor("w1", [D, FF], F32, kind="ExternalInput").ap()
    h_w2 = nc.dram_tensor("w2", [FF, D], F32, kind="ExternalInput").ap()
    h_iden = nc.dram_tensor("c_iden", [128, 128], BF16, kind="ExternalInput").ap()
    h_esel = nc.dram_tensor("c_esel", [NST, NST * 128], BF16,
                            kind="ExternalInput").ap()
    h_out = nc.dram_tensor("out", [T, D], F32, kind="ExternalOutput").ap()
    h_dbg = (nc.dram_tensor("dbg", [128, 16384], F32, kind="ExternalOutput").ap()
             if DBG is not None else None)

    with ExitStack() as ctx:
        def sb(name, shape, dt):
            return ctx.enter_context(nc.sbuf_tensor(name, shape, dt))

        def psum_t(name, shape, dt):
            return ctx.enter_context(nc.psum_tensor(name, shape, dt))

        x0 = sb("x0", [128, NTB, D], F32)
        x2 = sb("x2", [128, NTB, D], F32)
        xbig = sb("xbig", [128, NDB * (TQ + 4)], BF16)
        ncma = sb("ncma", [128, NMB, TQ], BF16)
        ncmb = sb("ncmb", [128, NMB, TQ], BF16)
        ntwi = sb("ntwi", [128, 4096], BF16)
        dAb = sb("dAb", [128, 2, 4, TQ + 1], F32)
        dBu = sb("dBu", [128, 2, 4, TQ + 1], BF16)
        stt = sb("stt", [128, 2, 4, TQ + 1], BF16)
        bbuf = sb("bbuf", [128, NST, TQ], BF16)
        cbuf = sb("cbuf", [128, NST, TQ], BF16)
        dlt = sb("dlt", [128, 2, TQ], BF16)
        du = sb("du", [128, TQ], BF16)
        sdu = sb("sdu", [128, TQ], F32)
        w_u = sb("w_u", [128, 2, NMB, 256], BF16)
        w_d = sb("w_d", [128, 2, D], BF16)
        h2 = sb("h2", [128, NFB, TQ], BF16)
        acc = sb("acc", [128, NTB, D], F32)
        rsl = sb("rsl", [128, 2, TQ], BF16)
        carry = sb("carry", [128, NDB, NST], F32)
        halo = sb("halo", [128, NDB, 4], BF16)
        w_dt = sb("w_dt", [RK, 2, 128], BF16)
        w_xp = sb("w_xp", [128, 2, RK + 2 * NST], BF16)
        w_g = sb("w_g", [128, NMB, NE], BF16)
        cwv = sb("cwv", [128, NDB, 4], F32)
        cbv = sb("cbv", [128, NDB], F32)
        dtbv = sb("dtbv", [128, NDB], F32)
        ddv = sb("ddv", [128, NDB], F32)
        a_co = sb("a_co", [128, NDB, NST], F32)
        nwb = sb("nwb", [128, D], BF16)
        iden = sb("iden", [128, 128], BF16)
        esel = sb("esel", [NST, NST * 128], BF16)
        scr = sb("scr", [128, D], F32)
        xrow = sb("xrow", [RK, TQ], BF16)
        brow = sb("brow", [NST, TQ], BF16)
        crow = sb("crow", [NST, TQ], BF16)
        rows = sb("rows", [128, 112], F32)
        gsc = sb("gsc", [128, 4, NTB, NE], F32)
        cst = sb("cst", [128, 2], F32)

        xi = xbig[:].rearrange("p (db t) -> p db t", db=NDB)
        ao_f = xbig[:].bitcast(F32)[:, 0:NTB * D].rearrange(
            "p (tb d) -> p tb d", tb=NTB)
        n_tok = ntwi[:].rearrange("p (tb d) -> p tb d", tb=NTB)
        w_in = ntwi[:].rearrange("p (s kb c) -> p s kb c", s=2, kb=NMB)
        w_out = ncmb[:].rearrange("p a b -> p (a b)").rearrange(
            "p (s d) -> p s d", s=4)
        grow = gsc[:, 0]
        et = gsc[:, 1]
        gmask = gsc[:, 2]
        cw = gsc[:, 3]

        def ao_dbs(tb):
            lo = tb * 2048
            hi = lo + 2048
            return [db for db in range(NDB)
                    if db * 516 < hi and (db + 1) * 516 > lo]

        pb = [psum_t(f"pb{i}", [128, 512], F32) for i in range(4)]
        pbig0 = psum_t("pbig0", [128, 1024], F32)
        pbig1 = psum_t("pbig1", [128, 1024], F32)
        pbt = [pb[0][:].bitcast(BF16), pb[1][:].bitcast(BF16)]

        sems = {}
        dma_sems = {}
        for e in Sched.ENGINES:
            sems[e] = ctx.enter_context(nc.semaphore(f"s_{e}"))
            dma_sems[e] = ctx.enter_context(nc.semaphore(f"d_{e}"))
        block = ctx.enter_context(nc.Block())
        S = Sched()

        def dma(queue, out_ap, in_ap, reads, writes, tag=""):
            S.add(queue, lambda g, o=out_ap, i=in_ap: g.dma_start(out=o, in_=i),
                  reads=reads, writes=writes, dma=True, tag=tag)

        def tap(name, ap2d, reads):
            if DBG == name:
                n = ap2d.free_size()
                dma("gpsimd", h_dbg[:, 0:n], ap2d, reads,
                    [("dbgout", 0)], f"tap.{name}")

        def V(emit, reads, writes, tag=""):
            return S.add("vector", emit, reads, writes, tag=tag)

        def A(emit, reads, writes, tag=""):
            return S.add("scalar", emit, reads, writes, tag=tag)

        def TE(emit, reads, writes, tag=""):
            return S.add("tensor", emit, reads, writes, tag=tag)

        def K(name, *k):
            return (name, k)

        dma("sync", iden[:], h_iden, (), [K("iden")], "iden")
        dma("sync", esel[:], h_esel, (), [K("esel")], "esel")
        nwrow = ntwi[0:1, 0:D]
        dma("gpsimd", nwrow, h_nw.rearrange("(o d) -> o d", o=1),
            (), [K("ntw", 0), K("ntw", 1)], "nw")
        for kb in range(NMB):
            dma("gpsimd", w_g[:, kb, :], h_wg[kb * 128:(kb + 1) * 128, :],
                (), [K("w_g", kb)], f"wg{kb}")
        alog_v = scr[:, 0:NDB * NST].rearrange("p (db n) -> p db n", db=NDB)
        for db in range(NDB):
            sl = slice(db * 128, (db + 1) * 128)
            dma("sync", cwv[:, db, :], h_convw[sl, 0, :], (), [K("cwv", db)])
            dma("sync", cbv[:, db:db + 1],
                h_convb[sl].rearrange("(p o) -> p o", o=1), (), [K("cbv", db)])
            dma("sync", dtbv[:, db:db + 1],
                h_dtb[sl].rearrange("(p o) -> p o", o=1), (), [K("dtbv", db)])
            dma("sync", ddv[:, db:db + 1],
                h_dd[sl].rearrange("(p o) -> p o", o=1), (), [K("ddv", db)])
            dma("sync", alog_v[:, db, :], h_alog[sl, :], (),
                [K("scr")], f"alog{db}")
        A(lambda s: s.activation(out=a_co[:], in_=alog_v[:, :, :], func=AF.Exp),
          [K("scr")], [K("a_co")], "A_exp")
        V(lambda v: v.tensor_scalar(out=a_co[:], in0=a_co[:], scalar1=-1.0,
                                    scalar2=None, op0=ALU.mult),
          [K("a_co")], [K("a_co")], "A_neg")
        for half in range(2):
            TE(lambda t, half=half: t.matmul(
                pbig0[:, half * 512:(half + 1) * 512], esel[0:1, 0:128],
                nwrow[:, half * 512:(half + 1) * 512], start=True, stop=True),
               [K("esel"), K("ntw", 0), K("ntw", 1)], [K("pbig0")],
               f"nwb_mm{half}")
        A(lambda s: s.activation(out=nwb[:], in_=pbig0[:, 0:D], func=AF.Copy),
          [K("pbig0")], [K("nwb")], "nwb_ev")
        V(lambda v: v.memset(carry[:], 0.0), (),
          [K("carry", db) for db in range(NDB)], "carry0")
        V(lambda v: v.memset(cst[:, 0:1], EPS), (), [K("cst")], "cst0")
        V(lambda v: v.memset(cst[:, 1:2], 1.0), (), [K("cst")], "cst1")

        def ln_stats(src_of_tb, key_of_tb, rc, tag):
            for tb in range(NTB):
                A(lambda s, tb=tb: s.activation(
                    out=scr[:, 0:D], in_=src_of_tb(tb), func=AF.Square,
                    accum_out=rows[:, rc + 8 + tb:rc + 9 + tb]),
                  key_of_tb(tb), [K("scr"), K("rows", rc + 8 + tb)],
                  f"{tag}.sq{tb}")
                V(lambda v, tb=tb: v.tensor_reduce(
                    out=rows[:, rc + 12 + tb:rc + 13 + tb],
                    in_=src_of_tb(tb), op=ALU.add, axis=AX.X),
                  key_of_tb(tb), [K("rows", rc + 12 + tb)], f"{tag}.s{tb}")
            V(lambda v: v.tensor_scalar(
                out=rows[:, rc + 4:rc + 8], in0=rows[:, rc + 12:rc + 16],
                scalar1=1.0 / D, scalar2=None, op0=ALU.mult),
              [K("rows", rc + 12 + t) for t in range(4)],
              [K("rows", rc + 4 + t) for t in range(4)], f"{tag}.m")
            V(lambda v: v.scalar_tensor_tensor(
                out=rows[:, rc + 16:rc + 20], in0=rows[:, rc + 4:rc + 8],
                scalar=-1.0, in1=rows[:, rc + 4:rc + 8],
                op0=ALU.mult, op1=ALU.mult),
              [K("rows", rc + 4 + t) for t in range(4)],
              [K("rows", rc + 16 + t) for t in range(4)], f"{tag}.mm")
            V(lambda v: v.scalar_tensor_tensor(
                out=rows[:, rc + 16:rc + 20], in0=rows[:, rc + 8:rc + 12],
                scalar=1.0 / D, in1=rows[:, rc + 16:rc + 20],
                op0=ALU.mult, op1=ALU.add),
              [K("rows", rc + 8 + t) for t in range(4)] +
              [K("rows", rc + 16 + t) for t in range(4)],
              [K("rows", rc + 16 + t) for t in range(4)], f"{tag}.var")
            A(lambda s: s.activation(
                out=rows[:, rc + 16:rc + 20], in_=rows[:, rc + 16:rc + 20],
                func=AF.Ln, bias=cst[:, 0:1]),
              [K("rows", rc + 16 + t) for t in range(4)] + [K("cst")],
              [K("rows", rc + 16 + t) for t in range(4)], f"{tag}.ln")
            A(lambda s: s.activation(
                out=rows[:, rc:rc + 4], in_=rows[:, rc + 16:rc + 20],
                func=AF.Exp, scale=-0.5),
              [K("rows", rc + 16 + t) for t in range(4)],
              [K("rows", rc + t) for t in range(4)], f"{tag}.rstd")
            V(lambda v: v.scalar_tensor_tensor(
                out=rows[:, rc + 4:rc + 8], in0=rows[:, rc + 4:rc + 8],
                scalar=-1.0, in1=rows[:, rc:rc + 4],
                op0=ALU.mult, op1=ALU.mult),
              [K("rows", rc + t) for t in range(8)],
              [K("rows", rc + 4 + t) for t in range(4)], f"{tag}.nmr")

        def tx_transpose(src, src_key_of, dst, dst_name, tag):
            for mb in range(NMB):
                bank = mb % 2
                pkey = K(f"pb{bank}")
                for tb in range(NTB):
                    TE(lambda t, mb=mb, tb=tb, bank=bank: t.transpose(
                        pbt[bank][:, tb * 128:(tb + 1) * 128],
                        src[:, tb, mb * 128:(mb + 1) * 128], iden[:]),
                       src_key_of(tb) + [K("iden")], [pkey],
                       f"{tag}.t{mb}.{tb}")
                A(lambda s, mb=mb, bank=bank: s.activation(
                    out=dst[:, mb, :], in_=pbt[bank][:, 0:TQ], func=AF.Copy),
                  [pkey], [K(dst_name)], f"{tag}.ev{mb}")

        def emit_pre(q):
            q0 = q * TQ
            for tb in range(NTB):
                dma("sync", x0[:, tb, :],
                    h_x[q0 + tb * 128:q0 + (tb + 1) * 128, :],
                    (), [K("x0", tb)], f"x0.{q}.{tb}")
            ln_stats(lambda tb: x0[:, tb, :],
                     lambda tb: [K("x0", tb)], 0, f"ln1.{q}")
            for tb in range(NTB):
                V(lambda v, tb=tb: v.tensor_scalar(
                    out=n_tok[:, tb, :], in0=x0[:, tb, :],
                    scalar1=rows[:, tb:tb + 1], scalar2=rows[:, 4 + tb:5 + tb],
                    op0=ALU.mult, op1=ALU.add),
                  [K("x0", tb), K("rows", tb), K("rows", 4 + tb)],
                  [K("ntw", tb)], f"ln1.ap{q}.{tb}")
            for tb in range(NTB):
                A(lambda s, tb=tb: s.activation(
                    out=scr[:, 0:D], in_=n_tok[:, tb, :], func=AF.Square,
                    accum_out=rows[:, 20 + tb:21 + tb]),
                  [K("ntw", tb)], [K("scr"), K("rows", 20 + tb)],
                  f"rms{q}.sq{tb}")
            A(lambda s: s.activation(out=rows[:, 24:28], in_=rows[:, 20:24],
                                     func=AF.Ln, scale=1.0 / D,
                                     bias=cst[:, 0:1]),
              [K("rows", 20 + t) for t in range(4)] + [K("cst")],
              [K("rows", 24 + t) for t in range(4)], f"rms{q}.ln")
            A(lambda s: s.activation(out=rows[:, 24:28], in_=rows[:, 24:28],
                                     func=AF.Exp, scale=-0.5),
              [K("rows", 24 + t) for t in range(4)],
              [K("rows", 24 + t) for t in range(4)], f"rms{q}.rinv")
            for tb in range(NTB):
                V(lambda v, tb=tb: v.tensor_scalar(
                    out=n_tok[:, tb, :], in0=n_tok[:, tb, :],
                    scalar1=rows[:, 24 + tb:25 + tb], scalar2=None,
                    op0=ALU.mult),
                  [K("ntw", tb), K("rows", 24 + tb)], [K("ntw", tb)],
                  f"rms{q}.m{tb}")
                V(lambda v, tb=tb: v.tensor_tensor(
                    out=n_tok[:, tb, :], in0=n_tok[:, tb, :], in1=nwb[:],
                    op=ALU.mult),
                  [K("ntw", tb), K("nwb")], [K("ntw", tb)], f"rms{q}.w{tb}")
            tx_transpose(n_tok, lambda tb: [K("ntw", tb)], ncma[:], "ncma",
                         f"tx1.{q}")
            if q == 0:
                tap("ncma", ncma[:].rearrange("p a b -> p (a b)"),
                    [K("ncma")])

        def win_dma(q, c, half, tag):
            slot = c % 2
            base = half * DI + c * 256
            src = h_win.rearrange("(kb p) n -> p kb n", p=128)[
                :, :, base:base + 256]
            dma("gpsimd", w_in[:, slot, :, :], src,
                (), [K("ntw", 2 * slot), K("ntw", 2 * slot + 1)],
                f"{tag}.{q}.{c}")

        def emit_inproj(q, part):
            for jb in range(4 * part, 4 * part + 4):
                c = jb // 2
                slot = c % 2
                if jb % 2 == 0:
                    win_dma(q, c, 0, "win")
                bank = pb[jb % 2]
                pkey = K(f"pb{jb % 2}")
                for kb in range(NMB):
                    TE(lambda t, slot=slot, jb=jb, kb=kb, bank=bank: t.matmul(
                        bank[:], w_in[:, slot, kb,
                                      (jb % 2) * 128:(jb % 2 + 1) * 128],
                        ncma[:, kb, :], start=(kb == 0), stop=(kb == NMB - 1)),
                       [K("ntw", 2 * slot), K("ntw", 2 * slot + 1),
                        K("ncma")], [pkey], f"ip{q}.{jb}.{kb}")
                db = jb
                A(lambda s, db=db, bank=bank: s.activation(
                    out=xi[:, db, 4:4 + TQ], in_=bank[:], func=AF.Copy),
                  [pkey], [K("xbig", db)], f"ipev{q}.{db}")
                if q > 0:
                    V(lambda v, db=db: v.tensor_copy(xi[:, db, 0:4],
                                                     halo[:, db, :]),
                      [K("halo", db)], [K("xbig", db)], f"hin{q}.{db}")
                else:
                    V(lambda v, db=db: v.memset(xi[:, db, 0:4], 0.0),
                      (), [K("xbig", db)], f"hz{q}.{db}")
                V(lambda v, db=db: v.tensor_scalar(
                    out=sdu[:], in0=xi[:, db, 1:1 + TQ],
                    scalar1=cwv[:, db, 0:1], scalar2=None, op0=ALU.mult),
                  [K("xbig", db), K("cwv", db)], [K("sdu")],
                  f"cv0.{q}.{db}")
                for k in (1, 2, 3):
                    V(lambda v, db=db, k=k: v.scalar_tensor_tensor(
                        out=sdu[:], in0=xi[:, db, 1 + k:1 + k + TQ],
                        scalar=cwv[:, db, k:k + 1], in1=sdu[:],
                        op0=ALU.mult, op1=ALU.add),
                      [K("xbig", db), K("cwv", db), K("sdu")],
                      [K("sdu")], f"cv{k}.{q}.{db}")
                V(lambda v, db=db: v.tensor_copy(halo[:, db, :],
                                                 xi[:, db, TQ:TQ + 4]),
                  [K("xbig", db)], [K("halo", db)], f"hsv{q}.{db}")
                A(lambda s, db=db: s.activation(
                    out=xi[:, db, 4:4 + TQ], in_=sdu[:], func=AF.Silu,
                    bias=cbv[:, db:db + 1]),
                  [K("sdu"), K("cbv", db), K("xbig", db)], [K("xbig", db)],
                  f"cvs{q}.{db}")
            if q == 0 and part == 3:
                tap("xi2", xbig[:],
                    [K("xbig", db) for db in range(NDB)])

        def emit_xproj(q):
            def xpw_dma(kb):
                dma("gpsimd", w_xp[:, kb % 2, :],
                    h_xpw[kb * 128:(kb + 1) * 128, :],
                    (), [K("w_xp", kb % 2)], f"xpw{q}.{kb}")

            groups = ((0, RK, 0, pb[0], K("pb0"), K("xrow"), xrow),
                      (RK, NST, 64, pb[0], K("pb0"), K("brow"), brow),
                      (RK + NST, NST, 0, pb[1], K("pb1"), K("crow"), crow))
            xpw_dma(0)
            xpw_dma(1)
            for kb in range(NDB):
                for (c0, nco, p0, bank, pkey, dkey, dst) in groups:
                    TE(lambda t, kb=kb, c0=c0, nco=nco, p0=p0, bank=bank:
                       t.matmul(bank[p0:p0 + nco, :],
                                w_xp[:, kb % 2, c0:c0 + nco],
                                xi[:, kb, 4:4 + TQ], start=(kb == 0),
                                stop=(kb == NDB - 1)),
                       [K("w_xp", kb % 2), K("xbig", kb)], [pkey],
                       f"xp{q}.{c0}.{kb}")
                if kb + 2 < NDB:
                    xpw_dma(kb + 2)
            for (c0, nco, p0, bank, pkey, dkey, dst) in groups:
                A(lambda s, p0=p0, nco=nco, bank=bank, dst=dst: s.activation(
                    out=dst[:, :], in_=bank[p0:p0 + nco, :], func=AF.Copy),
                  [pkey], [dkey], f"xpev{q}.{c0}")
            for (srw, skey, dbuf, dname) in ((brow, K("brow"), bbuf, "bbuf"),
                                             (crow, K("crow"), cbuf, "cbuf")):
                for n in range(NST):
                    bank = pb[n % 2]
                    pkey = K(f"pb{n % 2}")
                    TE(lambda t, n=n, bank=bank, srw=srw: t.matmul(
                        bank[:], esel[:, n * 128:(n + 1) * 128], srw[:, :],
                        start=True, stop=True),
                       [K("esel"), skey], [pkey], f"bc{dname}{q}.{n}")
                    A(lambda s, n=n, bank=bank, dbuf=dbuf: s.activation(
                        out=dbuf[:, n, :], in_=bank[:], func=AF.Copy),
                      [pkey], [K(dname, n)], f"bcv{dname}{q}.{n}")
            emit_dt(q, 0)
            if q == 0:
                tap("dlow", xrow[:, :], [K("xrow")])
                tap("bbuf", bbuf[:].rearrange("p a b -> p (a b)"),
                    [K("bbuf", n) for n in range(NST)])

        def emit_dt(q, db):
            bank = pb[db % 2]
            pkey = K(f"pb{db % 2}")
            sl = db % 2
            dma("gpsimd", w_dt[:, sl, :],
                h_dtw.rearrange("r (jb c) -> r jb c", c=128)[:, db, :],
                (), [K("w_dt", sl)], f"dtw{q}.{db}")
            TE(lambda t, sl=sl, bank=bank: t.matmul(
                bank[:], w_dt[:, sl, :], xrow[:, :], start=True, stop=True),
               [K("w_dt", sl), K("xrow")], [pkey], f"dt{q}.{db}")
            A(lambda s, db=db, bank=bank: s.activation(
                out=scr[:, 0:TQ], in_=bank[:], func=AF.Relu,
                bias=dtbv[:, db:db + 1]),
              [pkey, K("dtbv", db)], [K("scr")], f"dtr{q}.{db}")
            A(lambda s, db=db, bank=bank: s.activation(
                out=scr[:, TQ:2 * TQ], in_=bank[:], func=AF.Abs,
                bias=dtbv[:, db:db + 1]),
              [pkey, K("dtbv", db)], [K("scr")], f"dta{q}.{db}")
            A(lambda s: s.activation(out=scr[:, TQ:2 * TQ],
                                     in_=scr[:, TQ:2 * TQ],
                                     func=AF.Exp, scale=-1.0),
              [K("scr")], [K("scr")], f"dte{q}.{db}")
            A(lambda s: s.activation(out=scr[:, TQ:2 * TQ],
                                     in_=scr[:, TQ:2 * TQ],
                                     func=AF.Ln, bias=cst[:, 1:2]),
              [K("scr"), K("cst")], [K("scr")], f"dtl{q}.{db}")
            V(lambda v, sl=sl: v.tensor_tensor(
                out=dlt[:, sl, :], in0=scr[:, 0:TQ], in1=scr[:, TQ:2 * TQ],
                op=ALU.add),
              [K("scr")], [K("dlt", sl)], f"dtd{q}.{db}")

        def emit_scan_db(q, db):
            sl = db % 2
            if db + 1 < NDB:
                emit_dt(q, db + 1)
            V(lambda v, db=db, sl=sl: v.tensor_tensor(
                out=du[:], in0=dlt[:, sl, :], in1=xi[:, db, 4:4 + TQ],
                op=ALU.mult),
              [K("dlt", sl), K("xbig", db)], [K("du")], f"du{q}.{db}")
            V(lambda v, db=db: v.tensor_scalar(
                out=sdu[:], in0=xi[:, db, 4:4 + TQ], scalar1=ddv[:, db:db + 1],
                scalar2=None, op0=ALU.mult),
              [K("xbig", db), K("ddv", db)], [K("sdu")], f"yd{q}.{db}")
            for g in range(4):
                r = (db * 4 + g) % 2
                kdA = K("dAb", r)
                kdB = K("dBu", r)
                kst = K("stt", r)
                for i in range(4):
                    n = g * 4 + i
                    A(lambda s, db=db, r=r, i=i, n=n, sl=sl: s.activation(
                        out=dAb[:, r, i, 1:1 + TQ], in_=dlt[:, sl, :],
                        func=AF.Exp, scale=a_co[:, db, n:n + 1]),
                      [K("dlt", sl), K("a_co")], [kdA], f"dae{q}.{db}.{n}")
                V(lambda v, r=r: v.memset(dAb[:, r, :, 0:1], 0.0),
                  [kdA], [kdA], f"daz{q}.{db}.{g}")
                V(lambda v, db=db, r=r, g=g: v.tensor_copy(
                    dBu[:, r, :, 0:1],
                    carry[:, db, g * 4:(g + 1) * 4].unsqueeze(2)),
                  [K("carry", db)], [kdB], f"cin{q}.{db}.{g}")
                V(lambda v, r=r, g=g: v.tensor_tensor(
                    out=dBu[:, r, :, 1:1 + TQ],
                    in0=du[:].unsqueeze(1).broadcast_to([128, 4, TQ]),
                    in1=bbuf[:, g * 4:(g + 1) * 4, :], op=ALU.mult),
                  [K("du")] + [K("bbuf", g * 4 + i) for i in range(4)],
                  [kdB], f"dbu{q}.{db}.{g}")
                V(lambda v, r=r: v.tensor_tensor_scan(
                    stt[:, r].rearrange("p a b -> p (a b)"),
                    dAb[:, r].rearrange("p a b -> p (a b)"),
                    dBu[:, r].rearrange("p a b -> p (a b)"),
                    0.0, ALU.mult, ALU.add),
                  [kdA, kdB], [kst], f"tts{q}.{db}.{g}")
                V(lambda v, db=db, r=r, g=g: v.tensor_copy(
                    carry[:, db, g * 4:(g + 1) * 4].unsqueeze(2),
                    stt[:, r, :, TQ:TQ + 1]),
                  [kst], [K("carry", db)], f"csv{q}.{db}.{g}")
                V(lambda v, r=r, g=g: v.tensor_tensor(
                    out=stt[:, r, :, 1:1 + TQ], in0=stt[:, r, :, 1:1 + TQ],
                    in1=cbuf[:, g * 4:(g + 1) * 4, :], op=ALU.mult),
                  [kst] + [K("cbuf", g * 4 + i) for i in range(4)],
                  [kst], f"zc{q}.{db}.{g}")
                V(lambda v, r=r: v.tensor_tensor(
                    out=stt[:, r, 0:2, 1:1 + TQ], in0=stt[:, r, 0:2, 1:1 + TQ],
                    in1=stt[:, r, 2:4, 1:1 + TQ], op=ALU.add),
                  [kst], [kst], f"tr2.{q}.{db}.{g}")
                V(lambda v, r=r: v.tensor_tensor(
                    out=stt[:, r, 0:1, 1:1 + TQ], in0=stt[:, r, 0:1, 1:1 + TQ],
                    in1=stt[:, r, 1:2, 1:1 + TQ], op=ALU.add),
                  [kst], [kst], f"tr1.{q}.{db}.{g}")
                V(lambda v, r=r: v.tensor_tensor(
                    out=sdu[:], in0=sdu[:], in1=stt[:, r, 0, 1:1 + TQ],
                    op=ALU.add),
                  [K("sdu"), kst], [K("sdu")], f"ya{q}.{db}.{g}")
            V(lambda v, db=db: v.tensor_copy(xi[:, db, 4:4 + TQ], sdu[:]),
              [K("sdu"), K("xbig", db)], [K("xbig", db)], f"yw{q}.{db}")
            if q == 0 and db == NDB - 1:
                tap("yall", xbig[:], [K("xbig", d) for d in range(NDB)])

        def emit_res_db(q, rdb):
            c = rdb // 2
            slot = c % 2
            if rdb % 2 == 0:
                win_dma(q, c, 1, "wre")
            bank = pb[rdb % 2]
            pkey = K(f"pb{rdb % 2}")
            for kb in range(NMB):
                TE(lambda t, slot=slot, rdb=rdb, kb=kb, bank=bank:
                   t.matmul(bank[:],
                            w_in[:, slot, kb,
                                 (rdb % 2) * 128:(rdb % 2 + 1) * 128],
                            ncma[:, kb, :], start=(kb == 0),
                            stop=(kb == NMB - 1)),
                   [K("ntw", 2 * slot), K("ntw", 2 * slot + 1),
                    K("ncma")], [pkey], f"re{q}.{rdb}.{kb}")
            A(lambda s, rdb=rdb, bank=bank: s.activation(
                out=rsl[:, rdb % 2, :], in_=bank[:], func=AF.Silu),
              [pkey], [K("rsl", rdb % 2)], f"rev{q}.{rdb}")
            V(lambda v, rdb=rdb: v.tensor_tensor(
                out=xi[:, rdb, 4:4 + TQ], in0=xi[:, rdb, 4:4 + TQ],
                in1=rsl[:, rdb % 2, :], op=ALU.mult),
              [K("rsl", rdb % 2), K("xbig", rdb)], [K("xbig", rdb)],
              f"y2{q}.{rdb}")

        def emit_boundary(q):
            op_psum = {(0, 0): (pbig0[:, 0:512], K("pbig0")),
                       (0, 1): (pbig0[:, 512:1024], K("pbig0")),
                       (1, 0): (pbig1[:, 0:512], K("pbig1")),
                       (1, 1): (pbig1[:, 512:1024], K("pbig1")),
                       (2, 0): (pb[0][:, 0:512], K("pb0")),
                       (2, 1): (pb[1][:, 0:512], K("pb1")),
                       (3, 0): (pb[2][:, 0:512], K("pb2")),
                       (3, 1): (pb[3][:, 0:512], K("pb3"))}

            def wo_dma(kb):
                s_ = kb % 4
                dma("gpsimd", w_out[:, s_, :],
                    h_wout[kb * 128:(kb + 1) * 128, :],
                    (), [K("ncmb")],
                    f"wo{q}.{kb}")

            for kb in range(4):
                wo_dma(kb)
            for kb in range(NDB):
                s_ = kb % 4
                for tb in range(NTB):
                    for dmh in range(2):
                        dst, pkey = op_psum[(tb, dmh)]
                        TE(lambda t, kb=kb, tb=tb, dmh=dmh, dst=dst, s_=s_:
                           t.matmul(
                               dst, xi[:, kb, 4 + tb * 128:4 + (tb + 1) * 128],
                               w_out[:, s_, dmh * 512:(dmh + 1) * 512],
                               start=(kb == 0), stop=False),
                           [K("xbig", kb), K("ncmb")], [pkey],
                           f"op{q}.{kb}.{tb}.{dmh}")
                if kb + 4 < NDB:
                    wo_dma(kb + 4)
            for tb in range(NTB):
                for dmh in range(2):
                    sl = (tb * 2 + dmh) % 2
                    V(lambda v, tb=tb, dmh=dmh, sl=sl: v.tensor_scalar(
                        out=rsl[:, sl, :],
                        in0=x0[:, tb, dmh * 512:(dmh + 1) * 512],
                        scalar1=rows[:, tb:tb + 1],
                        scalar2=rows[:, 4 + tb:5 + tb],
                        op0=ALU.mult, op1=ALU.add),
                      [K("x0", tb), K("rows", tb), K("rows", 4 + tb)],
                      [K("rsl", sl)], f"x1r{q}.{tb}.{dmh}")
                    dst, pkey = op_psum[(tb, dmh)]
                    TE(lambda t, dst=dst, sl=sl: t.matmul(
                        dst, iden[:], rsl[:, sl, :],
                        start=False, stop=True),
                       [K("iden"), K("rsl", sl)], [pkey],
                       f"opx1.{q}.{tb}.{dmh}")
            aokeys = [K("xbig", db) for db in range(NDB)]
            for tb in range(NTB):
                for dmh in range(2):
                    src, pkey = op_psum[(tb, dmh)]
                    A(lambda s, tb=tb, dmh=dmh, src=src: s.activation(
                        out=ao_f[:, tb, dmh * 512:(dmh + 1) * 512], in_=src,
                        func=AF.Copy),
                      [pkey] + (aokeys if (tb, dmh) == (0, 0) else
                                [K("xbig", d) for d in ao_dbs(tb)]),
                      [K("xbig", d) for d in ao_dbs(tb)],
                      f"aoev{q}.{tb}.{dmh}")
            if q == 0:
                tap("ao", ao_f.rearrange("p a b -> p (a b)"),
                    [K("xbig", d) for d in range(NDB)])

            ln_stats(lambda tb: ao_f[:, tb, :],
                     lambda tb: [K("xbig", d) for d in ao_dbs(tb)],
                     28, f"ln3.{q}")
            for tb in range(NTB):
                V(lambda v, tb=tb: v.tensor_scalar(
                    out=scr[:, 0:D], in0=ao_f[:, tb, :],
                    scalar1=rows[:, 28 + tb:29 + tb],
                    scalar2=rows[:, 32 + tb:33 + tb],
                    op0=ALU.mult, op1=ALU.add),
                  [K("xbig", d) for d in ao_dbs(tb)] +
                  [K("rows", 28 + tb), K("rows", 32 + tb)],
                  [K("scr")], f"ln3a{q}.{tb}")
                V(lambda v, tb=tb: v.tensor_tensor(
                    out=x2[:, tb, :], in0=scr[:, 0:D],
                    in1=x0[:, tb, :], op=ALU.add),
                  [K("scr"), K("x0", tb)], [K("x2", tb)], f"ln3b{q}.{tb}")
            ln_stats(lambda tb: x2[:, tb, :],
                     lambda tb: [K("x2", tb)], 48, f"ln2.{q}")
            for tb in range(NTB):
                V(lambda v, tb=tb: v.tensor_scalar(
                    out=n_tok[:, tb, :], in0=x2[:, tb, :],
                    scalar1=rows[:, 48 + tb:49 + tb],
                    scalar2=rows[:, 52 + tb:53 + tb],
                    op0=ALU.mult, op1=ALU.add),
                  [K("x2", tb), K("rows", 48 + tb), K("rows", 52 + tb)],
                  [K("ntw", tb)], f"ln2a{q}.{tb}")
            if q == 0:
                tap("x2", x2[:].rearrange("p a b -> p (a b)"),
                    [K("x2", tb) for tb in range(NTB)])
            tx_transpose(n_tok, lambda tb: [K("ntw", tb)], ncmb[:], "ncmb",
                         f"tx2.{q}")
        def emit_gate(q):
            for tb in range(NTB):
                for kb in range(NMB):
                    TE(lambda t, tb=tb, kb=kb: t.matmul(
                        pb[2][:, 0:NE], ncmb[:, kb, tb * 128:(tb + 1) * 128],
                        w_g[:, kb, :], start=(kb == 0), stop=(kb == NMB - 1)),
                       [K("ncmb"), K("w_g", kb)], [K("pb2")],
                       f"g{q}.{tb}.{kb}")
                A(lambda s, tb=tb: s.activation(
                    out=grow[:, tb, :], in_=pb[2][:, 0:NE], func=AF.Copy),
                  [K("pb2")], [K("grow", tb)], f"gev{q}.{tb}")
            rg = 88
            for tb in range(NTB):
                V(lambda v, tb=tb: v.reduce_max(
                    out=rows[:, rg + tb:rg + tb + 1], in_=grow[:, tb, :],
                    axis=AX.X),
                  [K("grow", tb)], [K("rows", rg + tb)], f"gm{q}.{tb}")
                V(lambda v, tb=tb: v.tensor_scalar(
                    out=rows[:, rg + 4 + tb:rg + 5 + tb],
                    in0=rows[:, rg + tb:rg + tb + 1], scalar1=-1.0,
                    scalar2=None, op0=ALU.mult),
                  [K("rows", rg + tb)], [K("rows", rg + 4 + tb)],
                  f"gn{q}.{tb}")
                A(lambda s, tb=tb: s.activation(
                    out=et[:, tb, :], in_=grow[:, tb, :], func=AF.Exp,
                    bias=rows[:, rg + 4 + tb:rg + 5 + tb]),
                  [K("grow", tb), K("rows", rg + 4 + tb)], [K("et", tb)],
                  f"ge{q}.{tb}")
                V(lambda v, tb=tb: v.reduce_max(
                    out=rows[:, rg + 8 + tb:rg + 9 + tb], in_=et[:, tb, :],
                    axis=AX.X),
                  [K("et", tb)], [K("rows", rg + 8 + tb)], f"gm1{q}.{tb}")
                V(lambda v, tb=tb: v.tensor_scalar(
                    out=gmask[:, tb, :], in0=et[:, tb, :],
                    scalar1=rows[:, rg + 8 + tb:rg + 9 + tb], scalar2=None,
                    op0=ALU.is_equal),
                  [K("et", tb), K("rows", rg + 8 + tb)], [K("gmask", tb)],
                  f"gq1{q}.{tb}")
                V(lambda v, tb=tb: v.tensor_scalar(
                    out=grow[:, tb, :], in0=gmask[:, tb, :], scalar1=-1.0,
                    scalar2=1.0, op0=ALU.mult, op1=ALU.add),
                  [K("gmask", tb)], [K("grow", tb)], f"ga{q}.{tb}")
                V(lambda v, tb=tb: v.tensor_tensor(
                    out=grow[:, tb, :], in0=grow[:, tb, :], in1=et[:, tb, :],
                    op=ALU.mult),
                  [K("grow", tb), K("et", tb)], [K("grow", tb)],
                  f"ge2{q}.{tb}")
                V(lambda v, tb=tb: v.reduce_max(
                    out=rows[:, rg + 12 + tb:rg + 13 + tb],
                    in_=grow[:, tb, :], axis=AX.X),
                  [K("grow", tb)], [K("rows", rg + 12 + tb)], f"gm2{q}.{tb}")
                V(lambda v, tb=tb: v.tensor_tensor(
                    out=rows[:, rg + 16 + tb:rg + 17 + tb],
                    in0=rows[:, rg + 8 + tb:rg + 9 + tb],
                    in1=rows[:, rg + 12 + tb:rg + 13 + tb], op=ALU.add),
                  [K("rows", rg + 8 + tb), K("rows", rg + 12 + tb)],
                  [K("rows", rg + 16 + tb)], f"gs{q}.{tb}")
                V(lambda v, tb=tb: v.reciprocal(
                    out=rows[:, rg + 16 + tb:rg + 17 + tb],
                    in_=rows[:, rg + 16 + tb:rg + 17 + tb]),
                  [K("rows", rg + 16 + tb)], [K("rows", rg + 16 + tb)],
                  f"gr{q}.{tb}")
                V(lambda v, tb=tb: v.tensor_tensor(
                    out=rows[:, rg + 20 + tb:rg + 21 + tb],
                    in0=rows[:, rg + 8 + tb:rg + 9 + tb],
                    in1=rows[:, rg + 16 + tb:rg + 17 + tb], op=ALU.mult),
                  [K("rows", rg + 8 + tb), K("rows", rg + 16 + tb)],
                  [K("rows", rg + 20 + tb)], f"gw1{q}.{tb}")
                V(lambda v, tb=tb: v.tensor_scalar(
                    out=cw[:, tb, :], in0=grow[:, tb, :],
                    scalar1=rows[:, rg + 12 + tb:rg + 13 + tb], scalar2=None,
                    op0=ALU.is_equal),
                  [K("grow", tb), K("rows", rg + 12 + tb)], [K("cw", tb)],
                  f"gq2{q}.{tb}")
                V(lambda v, tb=tb: v.tensor_tensor(
                    out=rows[:, rg + 12 + tb:rg + 13 + tb],
                    in0=rows[:, rg + 12 + tb:rg + 13 + tb],
                    in1=rows[:, rg + 16 + tb:rg + 17 + tb], op=ALU.mult),
                  [K("rows", rg + 12 + tb), K("rows", rg + 16 + tb)],
                  [K("rows", rg + 12 + tb)], f"gw2{q}.{tb}")
                V(lambda v, tb=tb: v.tensor_scalar(
                    out=cw[:, tb, :], in0=cw[:, tb, :],
                    scalar1=rows[:, rg + 12 + tb:rg + 13 + tb], scalar2=None,
                    op0=ALU.mult),
                  [K("cw", tb), K("rows", rg + 12 + tb)], [K("cw", tb)],
                  f"gc2{q}.{tb}")
                V(lambda v, tb=tb: v.scalar_tensor_tensor(
                    out=cw[:, tb, :], in0=gmask[:, tb, :],
                    scalar=rows[:, rg + 20 + tb:rg + 21 + tb],
                    in1=cw[:, tb, :], op0=ALU.mult, op1=ALU.add),
                  [K("gmask", tb), K("rows", rg + 20 + tb), K("cw", tb)],
                  [K("cw", tb)], f"gc1{q}.{tb}")
            if q == 0:
                tap("cw", cw[:].rearrange("p a b -> p (a b)"),
                    [K("cw", tb) for tb in range(NTB)])

        def emit_expert_mu(q, e):
            wsrc = (h_wu[e] if e < NE else h_w1).rearrange(
                "(kb p) f -> p kb f", p=128)
            for c in range(8):
                slot = c % 2
                dma("gpsimd", w_u[:, slot, :, :],
                    wsrc[:, :, c * 256:(c + 1) * 256],
                    (), [K("w_u", slot)], f"wu{q}.{e}.{c}")
                for j in range(2):
                    jb = c * 2 + j
                    bank = pb[2 + (jb % 2)]
                    pkey = K(f"pb{2 + (jb % 2)}")
                    for kb in range(NMB):
                        TE(lambda t, slot=slot, j=j, kb=kb, bank=bank:
                           t.matmul(bank[:],
                                    w_u[:, slot, kb, j * 128:(j + 1) * 128],
                                    ncmb[:, kb, :], start=(kb == 0),
                                    stop=(kb == NMB - 1)),
                           [K("w_u", slot), K("ncmb")], [pkey],
                           f"mu{q}.{e}.{jb}.{kb}")
                    A(lambda s, jb=jb, bank=bank: s.activation(
                        out=h2[:, jb, :], in_=bank[:], func=AF.Gelu),
                      [pkey], [K("h2")], f"mug{q}.{e}.{jb}")
        def emit_expert_md(q, e):
            wdsrc = (h_wd[e] if e < NE else h_w2).rearrange(
                "(kb p) d -> p kb d", p=128)

            def wd_dma(f):
                dma("gpsimd", w_d[:, f % 2, :],
                    wdsrc[:, f, :],
                    (), [K("w_d", f % 2)], f"wd{q}.{e}.{f}")

            wd_dma(0)
            wd_dma(1)
            md_psum = {(0, 0): (pbig0[:, 0:512], K("pbig0")),
                       (0, 1): (pbig0[:, 512:1024], K("pbig0")),
                       (1, 0): (pbig1[:, 0:512], K("pbig1")),
                       (1, 1): (pbig1[:, 512:1024], K("pbig1")),
                       (2, 0): (pb[0][:, 0:512], K("pb0")),
                       (2, 1): (pb[1][:, 0:512], K("pb1")),
                       (3, 0): (pb[2][:, 0:512], K("pb2")),
                       (3, 1): (pb[3][:, 0:512], K("pb3"))}
            for kb in range(NFB):
                for tb in range(NTB):
                    for dmh in range(2):
                        dst, pkey = md_psum[(tb, dmh)]
                        TE(lambda t, kb=kb, tb=tb, dmh=dmh, dst=dst: t.matmul(
                            dst, h2[:, kb, tb * 128:(tb + 1) * 128],
                            w_d[:, kb % 2, dmh * 512:(dmh + 1) * 512],
                            start=(kb == 0), stop=(kb == NFB - 1)),
                           [K("h2"), K("w_d", kb % 2)], [pkey],
                           f"md{q}.{e}.{kb}.{tb}.{dmh}")
                if kb + 2 < NFB:
                    wd_dma(kb + 2)
            for (tb, dmh) in ((3, 0), (3, 1), (2, 0), (2, 1),
                              (0, 0), (0, 1), (1, 0), (1, 1)):
                src_, pkey = md_psum[(tb, dmh)]
                emit_acc_ev(q, e, tb, dmh, src_, pkey)

        def emit_acc_ev(q, e, tb, dmh, src_, pkey):
            dsl = slice(dmh * 512, (dmh + 1) * 512)
            if e == 0:
                V(lambda v: v.tensor_scalar(
                    out=acc[:, tb, dsl], in0=src_, scalar1=cw[:, tb, e:e + 1],
                    scalar2=None, op0=ALU.mult),
                  [pkey, K("cw", tb)], [K("acc", tb)], f"acv{q}.{e}.{tb}.{dmh}")
            elif e < NE:
                V(lambda v: v.scalar_tensor_tensor(
                    out=acc[:, tb, dsl], in0=src_, scalar=cw[:, tb, e:e + 1],
                    in1=acc[:, tb, dsl], op0=ALU.mult, op1=ALU.add),
                  [pkey, K("cw", tb), K("acc", tb)], [K("acc", tb)],
                  f"acv{q}.{e}.{tb}.{dmh}")
            else:
                V(lambda v: v.tensor_tensor(
                    out=acc[:, tb, dsl], in0=src_, in1=acc[:, tb, dsl],
                    op=ALU.add),
                  [pkey, K("acc", tb)], [K("acc", tb)], f"acv{q}.8.{tb}.{dmh}")

        def emit_ln4(q):
            q0 = q * TQ
            ln_stats(lambda tb: acc[:, tb, :],
                     lambda tb: [K("acc", tb)], 68, f"ln4.{q}")
            for tb in range(NTB):
                V(lambda v, tb=tb: v.tensor_scalar(
                    out=scr[:, 0:D], in0=acc[:, tb, :],
                    scalar1=rows[:, 68 + tb:69 + tb],
                    scalar2=rows[:, 72 + tb:73 + tb],
                    op0=ALU.mult, op1=ALU.add),
                  [K("acc", tb), K("rows", 68 + tb), K("rows", 72 + tb)],
                  [K("scr")], f"ln4a{q}.{tb}")
                V(lambda v, tb=tb: v.tensor_tensor(
                    out=acc[:, tb, :], in0=scr[:, 0:D],
                    in1=x2[:, tb, :], op=ALU.add),
                  [K("scr"), K("x2", tb), K("acc", tb)], [K("acc", tb)],
                  f"ln4b{q}.{tb}")
                dma("sync", h_out[q0 + tb * 128:q0 + (tb + 1) * 128, :],
                    acc[:, tb, :],
                    [K("acc", tb)], [K("hbm_out", (q, tb))],
                    f"out{q}.{tb}")

        def mamba_chunks(q):
            ch = [lambda q=q: emit_pre(q)]
            for part in range(4):
                ch.append(lambda q=q, part=part: emit_inproj(q, part))
            ch.append(lambda q=q: emit_xproj(q))
            for db in range(NDB):
                def both(q=q, db=db):
                    emit_scan_db(q, db)
                    emit_res_db(q, db)
                ch.append(both)
            return ch

        def moe_chunks(q):
            ch = []
            for e in range(NE + 1):
                ch.append(lambda q=q, e=e: emit_expert_mu(q, e))
                ch.append(lambda q=q, e=e: emit_expert_md(q, e))
            ch.append(lambda q=q: emit_ln4(q))
            return ch

        for ch in mamba_chunks(0):
            ch()
        for q in range(NQ):
            emit_boundary(q)
            mc = moe_chunks(q)
            mc.insert(1, lambda q=q: emit_gate(q))
            sc = mamba_chunks(q + 1) if q + 1 < NQ else []
            si = 0
            for i, m in enumerate(mc):
                tgt = (i + 1) * len(sc) // len(mc)
                while si < tgt:
                    sc[si]()
                    si += 1
                m()
            while si < len(sc):
                sc[si]()
                si += 1

        S.add("sync", lambda g: None,
              reads=[K("hbm_out", (q, tb)) for q in range(NQ)
                     for tb in range(NTB)],
              writes=(), tag="final_fence")

        S.finalize(block, sems, dma_sems)
    return nc


_NC_CACHE = {}


def _get_program():
    if "nc" not in _NC_CACHE:
        _NC_CACHE["nc"] = build_program()
    return _NC_CACHE["nc"]


def kernel(L0_attn_x, L0_attn_in_proj_w, L0_attn_conv_w, L0_attn_conv_b,
           L0_attn_x_proj_w, L0_attn_dt_proj_w, L0_attn_dt_proj_b,
           L0_attn_A_log, L0_attn_D, L0_attn_out_proj_w, L0_attn_norm_w,
           L0_ffn_Wg, L0_ffn_Wu, L0_ffn_Wd, L0_ffn_W1, L0_ffn_W2):
    import ml_dtypes
    x = np.ascontiguousarray(np.asarray(L0_attn_x, dtype=np.float32))
    iden_c = np.eye(128, dtype=ml_dtypes.bfloat16)
    esel_c = np.zeros((NST, NST * 128), dtype=ml_dtypes.bfloat16)
    for n in range(NST):
        esel_c[n, n * 128:(n + 1) * 128] = 1.0

    def f32(a):
        return np.ascontiguousarray(np.asarray(a, dtype=np.float32))

    shared = {
        "w_in": f32(L0_attn_in_proj_w),
        "conv_w": f32(L0_attn_conv_w),
        "conv_b": f32(L0_attn_conv_b),
        "x_proj_w": f32(L0_attn_x_proj_w),
        "dt_w": f32(L0_attn_dt_proj_w),
        "dt_b": f32(L0_attn_dt_proj_b),
        "a_log": f32(L0_attn_A_log),
        "d_vec": f32(L0_attn_D),
        "w_out": f32(L0_attn_out_proj_w),
        "norm_w": f32(L0_attn_norm_w),
        "wg": f32(L0_ffn_Wg),
        "wu": f32(L0_ffn_Wu),
        "wd": f32(L0_ffn_Wd),
        "w1": f32(L0_ffn_W1),
        "w2": f32(L0_ffn_W2),
        "c_iden": iden_c,
        "c_esel": esel_c,
    }
    nc = _get_program()
    in_maps = []
    for b in range(B):
        m = dict(shared)
        m["x"] = np.ascontiguousarray(x[b])
        in_maps.append(m)
    res = run_bass_kernel_spmd(nc, in_maps, core_ids=list(range(B)))
    out = np.stack([np.asarray(res.results[b]["out"]) for b in range(B)],
                   axis=0)
    if DBG is not None:
        return out.astype(np.float32), np.asarray(res.results[0]["dbg"])
    return out.astype(np.float32)


# version 13
# speedup vs baseline: 1.1210x; 1.1210x over previous
import numpy as np
from contextlib import ExitStack

import concourse.bass as bass
import concourse.mybir as mybir
from concourse.bass_utils import run_bass_kernel_spmd

F32 = mybir.dt.float32
BF16 = mybir.dt.bfloat16
AF = mybir.ActivationFunctionType
ALU = mybir.AluOpType
AX = mybir.AxisListType

B = 8
T = 2048
D = 1024
DI = 2048
NST = 16
RK = 64
NDB = DI // 128
NMB = D // 128
NE = 8
FF = 2048
NFB = FF // 128
EPS = 1e-5
TQ = 512
NQ = T // TQ
NTB = TQ // 128

ALL = ("*ALL*",)
DBG = None


class Sched:

    ENGINES = ("tensor", "vector", "scalar", "gpsimd", "sync")

    def __init__(self):
        self.ops = []
        self.regions = {}

    def add(self, engine, emit, reads=(), writes=(), dma=False, tag=""):
        idx = len(self.ops)
        op = dict(i=idx, eng=engine, emit=emit, dma=dma, tag=tag, deps=set())
        reads = list(reads)
        writes = list(writes)

        def conflicts(name, sub):
            d = self.regions.get(name)
            if d is None:
                return []
            if sub is ALL:
                return list(d.keys())
            out = []
            if sub in d:
                out.append(sub)
            if ALL in d:
                out.append(ALL)
            return out

        for (name, sub) in reads:
            for k in conflicts(name, sub):
                w, _ = self.regions[name][k]
                if w is not None:
                    op["deps"].add(w["i"])
        for (name, sub) in writes:
            for k in conflicts(name, sub):
                w, rs = self.regions[name][k]
                if w is not None:
                    op["deps"].add(w["i"])
                for r in rs:
                    op["deps"].add(r["i"])
        for (name, sub) in reads:
            d = self.regions.setdefault(name, {})
            if sub not in d:
                d[sub] = (None, [])
            d[sub][1].append(op)
        for (name, sub) in writes:
            d = self.regions.setdefault(name, {})
            if sub is ALL:
                for k in list(d.keys()):
                    d[k] = (op, [])
                d[ALL] = (op, [])
            else:
                d[sub] = (op, [])
        self.ops.append(op)
        return op

    def finalize(self, block, sems, dma_sems):
        ops = self.ops
        pos = {e: 0 for e in self.ENGINES}
        for op in ops:
            op["pos"] = pos[op["eng"]]
            pos[op["eng"]] += 1
        for op in ops:
            op["deps"] = {d for d in op["deps"]
                          if ops[d]["dma"] or ops[d]["eng"] != op["eng"]
                          or (op["eng"] != "tensor"
                              and (op["pos"] - ops[d]["pos"]) <= 6)}
        waited = set()
        for op in ops:
            waited.update(op["deps"])
        cnt = {e: 0 for e in self.ENGINES}
        dcnt = {e: 0 for e in self.ENGINES}
        for op in ops:
            if op["dma"]:
                dcnt[op["eng"]] += 1
                op["count"] = dcnt[op["eng"]] * 16
                op["inc"] = True
            else:
                op["inc"] = op["i"] in waited
                if op["inc"]:
                    cnt[op["eng"]] += 1
                op["count"] = cnt[op["eng"]]

        def make_body(eng_name):
            mine = [op for op in ops if op["eng"] == eng_name]

            def body(eng):
                seen = {}
                for op in mine:
                    needs = {}
                    for d in op["deps"]:
                        dop = ops[d]
                        key = (dop["eng"], dop["dma"])
                        if needs.get(key, 0) < dop["count"]:
                            needs[key] = dop["count"]
                    for key in sorted(needs, key=lambda k: (k[0], k[1])):
                        v = needs[key]
                        if seen.get(key, 0) >= v:
                            continue
                        tgt, is_dma = key
                        eng.wait_ge(dma_sems[tgt] if is_dma else sems[tgt], v)
                        seen[key] = v
                    inst = op["emit"](eng)
                    if op["inc"]:
                        assert inst is not None, f"op {op['tag']} needs inst"
                        if op["dma"]:
                            inst.then_inc(dma_sems[eng_name], 16)
                        else:
                            inst.then_inc(sems[eng_name], 1)
            return body

        for name in self.ENGINES:
            getattr(block, name)(make_body(name))


def build_program():
    nc = bass.Bass("TRN2", target_bir_lowering=False)

    h_x = nc.dram_tensor("x", [T, D], F32, kind="ExternalInput").ap()
    h_win = nc.dram_tensor("w_in", [D, 2 * DI], F32, kind="ExternalInput").ap()
    h_convw = nc.dram_tensor("conv_w", [DI, 1, 4], F32, kind="ExternalInput").ap()
    h_convb = nc.dram_tensor("conv_b", [DI], F32, kind="ExternalInput").ap()
    h_xpw = nc.dram_tensor("x_proj_w", [DI, RK + 2 * NST], F32,
                           kind="ExternalInput").ap()
    h_dtw = nc.dram_tensor("dt_w", [RK, DI], F32, kind="ExternalInput").ap()
    h_dtb = nc.dram_tensor("dt_b", [DI], F32, kind="ExternalInput").ap()
    h_alog = nc.dram_tensor("a_log", [DI, NST], F32, kind="ExternalInput").ap()
    h_dd = nc.dram_tensor("d_vec", [DI], F32, kind="ExternalInput").ap()
    h_wout = nc.dram_tensor("w_out", [DI, D], F32, kind="ExternalInput").ap()
    h_nw = nc.dram_tensor("norm_w", [D], F32, kind="ExternalInput").ap()
    h_wg = nc.dram_tensor("wg", [D, NE], F32, kind="ExternalInput").ap()
    h_wu = nc.dram_tensor("wu", [NE, D, FF], F32, kind="ExternalInput").ap()
    h_wd = nc.dram_tensor("wd", [NE, FF, D], F32, kind="ExternalInput").ap()
    h_w1 = nc.dram_tensor("w1", [D, FF], F32, kind="ExternalInput").ap()
    h_w2 = nc.dram_tensor("w2", [FF, D], F32, kind="ExternalInput").ap()
    h_iden = nc.dram_tensor("c_iden", [128, 128], BF16, kind="ExternalInput").ap()
    h_esel = nc.dram_tensor("c_esel", [NST, NST * 128], BF16,
                            kind="ExternalInput").ap()
    h_out = nc.dram_tensor("out", [T, D], F32, kind="ExternalOutput").ap()
    h_dbg = (nc.dram_tensor("dbg", [128, 16384], F32, kind="ExternalOutput").ap()
             if DBG is not None else None)

    with ExitStack() as ctx:
        def sb(name, shape, dt):
            return ctx.enter_context(nc.sbuf_tensor(name, shape, dt))

        def psum_t(name, shape, dt):
            return ctx.enter_context(nc.psum_tensor(name, shape, dt))

        x0 = sb("x0", [128, NTB, D], F32)
        x2 = sb("x2", [128, NTB, D], F32)
        xbig = sb("xbig", [128, NDB * (TQ + 4)], BF16)
        ncma = sb("ncma", [128, NMB, TQ], BF16)
        ncmb = sb("ncmb", [128, NMB, TQ], BF16)
        ntwi = sb("ntwi", [128, 4096], BF16)
        dAb = sb("dAb", [128, 2, 4, TQ + 1], F32)
        dBu = sb("dBu", [128, 2, 4, TQ + 1], BF16)
        stt = sb("stt", [128, 2, 4, TQ + 1], BF16)
        bbuf = sb("bbuf", [128, NST, TQ], BF16)
        cbuf = sb("cbuf", [128, NST, TQ], BF16)
        dlt = sb("dlt", [128, 2, TQ], BF16)
        du = sb("du", [128, TQ], BF16)
        sdu = sb("sdu", [128, TQ], F32)
        w_u = sb("w_u", [128, 2, NMB, 256], BF16)
        w_d = sb("w_d", [128, 2, D], BF16)
        h2 = sb("h2", [128, NFB, TQ], BF16)
        acc = sb("acc", [128, NTB, D], F32)
        rsl = sb("rsl", [128, 2, TQ], BF16)
        carry = sb("carry", [128, NDB, NST], F32)
        halo = sb("halo", [128, NDB, 4], BF16)
        w_dt = sb("w_dt", [RK, 2, 128], BF16)
        w_xp = sb("w_xp", [128, 2, RK + 2 * NST], BF16)
        w_g = sb("w_g", [128, NMB, NE], BF16)
        cwv = sb("cwv", [128, NDB, 4], F32)
        cbv = sb("cbv", [128, NDB], F32)
        dtbv = sb("dtbv", [128, NDB], F32)
        ddv = sb("ddv", [128, NDB], F32)
        a_co = sb("a_co", [128, NDB, NST], F32)
        nwb = sb("nwb", [128, D], BF16)
        iden = sb("iden", [128, 128], BF16)
        esel = sb("esel", [NST, NST * 128], BF16)
        scr = sb("scr", [128, D], F32)
        xrow = sb("xrow", [RK, TQ], BF16)
        brow = sb("brow", [NST, TQ], BF16)
        crow = sb("crow", [NST, TQ], BF16)
        rows = sb("rows", [128, 112], F32)
        gsc = sb("gsc", [128, 4, NTB, NE], F32)
        cst = sb("cst", [128, 2], F32)

        xi = xbig[:].rearrange("p (db t) -> p db t", db=NDB)
        ao_f = xbig[:].bitcast(F32)[:, 0:NTB * D].rearrange(
            "p (tb d) -> p tb d", tb=NTB)
        n_tok = ntwi[:].rearrange("p (tb d) -> p tb d", tb=NTB)
        w_in = ntwi[:].rearrange("p (s kb c) -> p s kb c", s=2, kb=NMB)
        w_out = ncmb[:].rearrange("p a b -> p (a b)").rearrange(
            "p (s d) -> p s d", s=4)
        grow = gsc[:, 0]
        et = gsc[:, 1]
        gmask = gsc[:, 2]
        cw = gsc[:, 3]

        def ao_dbs(tb):
            lo = tb * 2048
            hi = lo + 2048
            return [db for db in range(NDB)
                    if db * 516 < hi and (db + 1) * 516 > lo]

        pb = [psum_t(f"pb{i}", [128, 512], F32) for i in range(4)]
        pbig0 = psum_t("pbig0", [128, 1024], F32)
        pbig1 = psum_t("pbig1", [128, 1024], F32)
        pbt = [pb[0][:].bitcast(BF16), pb[1][:].bitcast(BF16)]

        sems = {}
        dma_sems = {}
        for e in Sched.ENGINES:
            sems[e] = ctx.enter_context(nc.semaphore(f"s_{e}"))
            dma_sems[e] = ctx.enter_context(nc.semaphore(f"d_{e}"))
        block = ctx.enter_context(nc.Block())
        S = Sched()

        def dma(queue, out_ap, in_ap, reads, writes, tag=""):
            S.add(queue, lambda g, o=out_ap, i=in_ap: g.dma_start(out=o, in_=i),
                  reads=reads, writes=writes, dma=True, tag=tag)

        def tap(name, ap2d, reads):
            if DBG == name:
                n = ap2d.free_size()
                dma("gpsimd", h_dbg[:, 0:n], ap2d, reads,
                    [("dbgout", 0)], f"tap.{name}")

        def V(emit, reads, writes, tag=""):
            return S.add("vector", emit, reads, writes, tag=tag)

        def A(emit, reads, writes, tag=""):
            return S.add("scalar", emit, reads, writes, tag=tag)

        def TE(emit, reads, writes, tag=""):
            return S.add("tensor", emit, reads, writes, tag=tag)

        def K(name, *k):
            return (name, k)

        dma("sync", iden[:], h_iden, (), [K("iden")], "iden")
        dma("sync", esel[:], h_esel, (), [K("esel")], "esel")
        nwrow = ntwi[0:1, 0:D]
        dma("gpsimd", nwrow, h_nw.rearrange("(o d) -> o d", o=1),
            (), [K("ntw", 0), K("ntw", 1)], "nw")
        for kb in range(NMB):
            dma("gpsimd", w_g[:, kb, :], h_wg[kb * 128:(kb + 1) * 128, :],
                (), [K("w_g", kb)], f"wg{kb}")
        alog_v = scr[:, 0:NDB * NST].rearrange("p (db n) -> p db n", db=NDB)
        for db in range(NDB):
            sl = slice(db * 128, (db + 1) * 128)
            dma("sync", cwv[:, db, :], h_convw[sl, 0, :], (), [K("cwv", db)])
            dma("sync", cbv[:, db:db + 1],
                h_convb[sl].rearrange("(p o) -> p o", o=1), (), [K("cbv", db)])
            dma("sync", dtbv[:, db:db + 1],
                h_dtb[sl].rearrange("(p o) -> p o", o=1), (), [K("dtbv", db)])
            dma("sync", ddv[:, db:db + 1],
                h_dd[sl].rearrange("(p o) -> p o", o=1), (), [K("ddv", db)])
            dma("sync", alog_v[:, db, :], h_alog[sl, :], (),
                [K("scr")], f"alog{db}")
        A(lambda s: s.activation(out=a_co[:], in_=alog_v[:, :, :], func=AF.Exp),
          [K("scr")], [K("a_co")], "A_exp")
        V(lambda v: v.tensor_scalar(out=a_co[:], in0=a_co[:], scalar1=-1.0,
                                    scalar2=None, op0=ALU.mult),
          [K("a_co")], [K("a_co")], "A_neg")
        for half in range(2):
            TE(lambda t, half=half: t.matmul(
                pbig0[:, half * 512:(half + 1) * 512], esel[0:1, 0:128],
                nwrow[:, half * 512:(half + 1) * 512], start=True, stop=True),
               [K("esel"), K("ntw", 0), K("ntw", 1)], [K("pbig0")],
               f"nwb_mm{half}")
        A(lambda s: s.activation(out=nwb[:], in_=pbig0[:, 0:D], func=AF.Copy),
          [K("pbig0")], [K("nwb")], "nwb_ev")
        V(lambda v: v.memset(carry[:], 0.0), (),
          [K("carry", db) for db in range(NDB)], "carry0")
        V(lambda v: v.memset(cst[:, 0:1], EPS), (), [K("cst")], "cst0")
        V(lambda v: v.memset(cst[:, 1:2], 1.0), (), [K("cst")], "cst1")

        def ln_stats(src_of_tb, key_of_tb, rc, tag):
            for tb in range(NTB):
                A(lambda s, tb=tb: s.activation(
                    out=scr[:, 0:D], in_=src_of_tb(tb), func=AF.Square,
                    accum_out=rows[:, rc + 8 + tb:rc + 9 + tb]),
                  key_of_tb(tb), [K("scr"), K("rows", rc + 8 + tb)],
                  f"{tag}.sq{tb}")
                V(lambda v, tb=tb: v.tensor_reduce(
                    out=rows[:, rc + 12 + tb:rc + 13 + tb],
                    in_=src_of_tb(tb), op=ALU.add, axis=AX.X),
                  key_of_tb(tb), [K("rows", rc + 12 + tb)], f"{tag}.s{tb}")
            V(lambda v: v.tensor_scalar(
                out=rows[:, rc + 4:rc + 8], in0=rows[:, rc + 12:rc + 16],
                scalar1=1.0 / D, scalar2=None, op0=ALU.mult),
              [K("rows", rc + 12 + t) for t in range(4)],
              [K("rows", rc + 4 + t) for t in range(4)], f"{tag}.m")
            V(lambda v: v.scalar_tensor_tensor(
                out=rows[:, rc + 16:rc + 20], in0=rows[:, rc + 4:rc + 8],
                scalar=-1.0, in1=rows[:, rc + 4:rc + 8],
                op0=ALU.mult, op1=ALU.mult),
              [K("rows", rc + 4 + t) for t in range(4)],
              [K("rows", rc + 16 + t) for t in range(4)], f"{tag}.mm")
            V(lambda v: v.scalar_tensor_tensor(
                out=rows[:, rc + 16:rc + 20], in0=rows[:, rc + 8:rc + 12],
                scalar=1.0 / D, in1=rows[:, rc + 16:rc + 20],
                op0=ALU.mult, op1=ALU.add),
              [K("rows", rc + 8 + t) for t in range(4)] +
              [K("rows", rc + 16 + t) for t in range(4)],
              [K("rows", rc + 16 + t) for t in range(4)], f"{tag}.var")
            A(lambda s: s.activation(
                out=rows[:, rc + 16:rc + 20], in_=rows[:, rc + 16:rc + 20],
                func=AF.Ln, bias=cst[:, 0:1]),
              [K("rows", rc + 16 + t) for t in range(4)] + [K("cst")],
              [K("rows", rc + 16 + t) for t in range(4)], f"{tag}.ln")
            A(lambda s: s.activation(
                out=rows[:, rc:rc + 4], in_=rows[:, rc + 16:rc + 20],
                func=AF.Exp, scale=-0.5),
              [K("rows", rc + 16 + t) for t in range(4)],
              [K("rows", rc + t) for t in range(4)], f"{tag}.rstd")
            V(lambda v: v.scalar_tensor_tensor(
                out=rows[:, rc + 4:rc + 8], in0=rows[:, rc + 4:rc + 8],
                scalar=-1.0, in1=rows[:, rc:rc + 4],
                op0=ALU.mult, op1=ALU.mult),
              [K("rows", rc + t) for t in range(8)],
              [K("rows", rc + 4 + t) for t in range(4)], f"{tag}.nmr")

        def tx_transpose(src, src_key_of, dst, dst_name, tag):
            for mb in range(NMB):
                bank = mb % 2
                pkey = K(f"pb{bank}")
                for tb in range(NTB):
                    TE(lambda t, mb=mb, tb=tb, bank=bank: t.transpose(
                        pbt[bank][:, tb * 128:(tb + 1) * 128],
                        src[:, tb, mb * 128:(mb + 1) * 128], iden[:]),
                       src_key_of(tb) + [K("iden")], [pkey],
                       f"{tag}.t{mb}.{tb}")
                A(lambda s, mb=mb, bank=bank: s.activation(
                    out=dst[:, mb, :], in_=pbt[bank][:, 0:TQ], func=AF.Copy),
                  [pkey], [K(dst_name)], f"{tag}.ev{mb}")

        def emit_pre(q):
            q0 = q * TQ
            for tb in range(NTB):
                dma("sync", x0[:, tb, :],
                    h_x[q0 + tb * 128:q0 + (tb + 1) * 128, :],
                    (), [K("x0", tb)], f"x0.{q}.{tb}")
            ln_stats(lambda tb: x0[:, tb, :],
                     lambda tb: [K("x0", tb)], 0, f"ln1.{q}")
            for tb in range(NTB):
                V(lambda v, tb=tb: v.tensor_scalar(
                    out=n_tok[:, tb, :], in0=x0[:, tb, :],
                    scalar1=rows[:, tb:tb + 1], scalar2=rows[:, 4 + tb:5 + tb],
                    op0=ALU.mult, op1=ALU.add),
                  [K("x0", tb), K("rows", tb), K("rows", 4 + tb)],
                  [K("ntw", tb)], f"ln1.ap{q}.{tb}")
            for tb in range(NTB):
                A(lambda s, tb=tb: s.activation(
                    out=scr[:, 0:D], in_=n_tok[:, tb, :], func=AF.Square,
                    accum_out=rows[:, 20 + tb:21 + tb]),
                  [K("ntw", tb)], [K("scr"), K("rows", 20 + tb)],
                  f"rms{q}.sq{tb}")
            A(lambda s: s.activation(out=rows[:, 24:28], in_=rows[:, 20:24],
                                     func=AF.Ln, scale=1.0 / D,
                                     bias=cst[:, 0:1]),
              [K("rows", 20 + t) for t in range(4)] + [K("cst")],
              [K("rows", 24 + t) for t in range(4)], f"rms{q}.ln")
            A(lambda s: s.activation(out=rows[:, 24:28], in_=rows[:, 24:28],
                                     func=AF.Exp, scale=-0.5),
              [K("rows", 24 + t) for t in range(4)],
              [K("rows", 24 + t) for t in range(4)], f"rms{q}.rinv")
            for tb in range(NTB):
                V(lambda v, tb=tb: v.tensor_scalar(
                    out=n_tok[:, tb, :], in0=n_tok[:, tb, :],
                    scalar1=rows[:, 24 + tb:25 + tb], scalar2=None,
                    op0=ALU.mult),
                  [K("ntw", tb), K("rows", 24 + tb)], [K("ntw", tb)],
                  f"rms{q}.m{tb}")
                V(lambda v, tb=tb: v.tensor_tensor(
                    out=n_tok[:, tb, :], in0=n_tok[:, tb, :], in1=nwb[:],
                    op=ALU.mult),
                  [K("ntw", tb), K("nwb")], [K("ntw", tb)], f"rms{q}.w{tb}")
            tx_transpose(n_tok, lambda tb: [K("ntw", tb)], ncma[:], "ncma",
                         f"tx1.{q}")
            if q == 0:
                tap("ncma", ncma[:].rearrange("p a b -> p (a b)"),
                    [K("ncma")])

        def win_dma(q, c, half, tag):
            slot = c % 2
            base = half * DI + c * 256
            src = h_win.rearrange("(kb p) n -> p kb n", p=128)[
                :, :, base:base + 256]
            dma("gpsimd", w_in[:, slot, :, :], src,
                (), [K("ntw", 2 * slot), K("ntw", 2 * slot + 1)],
                f"{tag}.{q}.{c}")

        def emit_inproj(q, part):
            for jb in range(4 * part, 4 * part + 4):
                c = jb // 2
                slot = c % 2
                if jb % 2 == 0:
                    win_dma(q, c, 0, "win")
                bank = pb[jb % 2]
                pkey = K(f"pb{jb % 2}")
                for kb in range(NMB):
                    TE(lambda t, slot=slot, jb=jb, kb=kb, bank=bank: t.matmul(
                        bank[:], w_in[:, slot, kb,
                                      (jb % 2) * 128:(jb % 2 + 1) * 128],
                        ncma[:, kb, :], start=(kb == 0), stop=(kb == NMB - 1)),
                       [K("ntw", 2 * slot), K("ntw", 2 * slot + 1),
                        K("ncma")], [pkey], f"ip{q}.{jb}.{kb}")
                db = jb
                A(lambda s, db=db, bank=bank: s.activation(
                    out=xi[:, db, 4:4 + TQ], in_=bank[:], func=AF.Copy),
                  [pkey], [K("xbig", db)], f"ipev{q}.{db}")
                if q > 0:
                    V(lambda v, db=db: v.tensor_copy(xi[:, db, 0:4],
                                                     halo[:, db, :]),
                      [K("halo", db)], [K("xbig", db)], f"hin{q}.{db}")
                else:
                    V(lambda v, db=db: v.memset(xi[:, db, 0:4], 0.0),
                      (), [K("xbig", db)], f"hz{q}.{db}")
                V(lambda v, db=db: v.tensor_scalar(
                    out=sdu[:], in0=xi[:, db, 1:1 + TQ],
                    scalar1=cwv[:, db, 0:1], scalar2=None, op0=ALU.mult),
                  [K("xbig", db), K("cwv", db)], [K("sdu")],
                  f"cv0.{q}.{db}")
                for k in (1, 2, 3):
                    V(lambda v, db=db, k=k: v.scalar_tensor_tensor(
                        out=sdu[:], in0=xi[:, db, 1 + k:1 + k + TQ],
                        scalar=cwv[:, db, k:k + 1], in1=sdu[:],
                        op0=ALU.mult, op1=ALU.add),
                      [K("xbig", db), K("cwv", db), K("sdu")],
                      [K("sdu")], f"cv{k}.{q}.{db}")
                V(lambda v, db=db: v.tensor_copy(halo[:, db, :],
                                                 xi[:, db, TQ:TQ + 4]),
                  [K("xbig", db)], [K("halo", db)], f"hsv{q}.{db}")
                A(lambda s, db=db: s.activation(
                    out=xi[:, db, 4:4 + TQ], in_=sdu[:], func=AF.Silu,
                    bias=cbv[:, db:db + 1]),
                  [K("sdu"), K("cbv", db), K("xbig", db)], [K("xbig", db)],
                  f"cvs{q}.{db}")
            if q == 0 and part == 3:
                tap("xi2", xbig[:],
                    [K("xbig", db) for db in range(NDB)])

        def emit_xproj(q):
            def xpw_dma(kb):
                dma("gpsimd", w_xp[:, kb % 2, :],
                    h_xpw[kb * 128:(kb + 1) * 128, :],
                    (), [K("w_xp", kb % 2)], f"xpw{q}.{kb}")

            groups = ((0, RK, 0, pb[0], K("pb0"), K("xrow"), xrow),
                      (RK, NST, 64, pb[0], K("pb0"), K("brow"), brow),
                      (RK + NST, NST, 0, pb[1], K("pb1"), K("crow"), crow))
            xpw_dma(0)
            xpw_dma(1)
            for kb in range(NDB):
                for (c0, nco, p0, bank, pkey, dkey, dst) in groups:
                    TE(lambda t, kb=kb, c0=c0, nco=nco, p0=p0, bank=bank:
                       t.matmul(bank[p0:p0 + nco, :],
                                w_xp[:, kb % 2, c0:c0 + nco],
                                xi[:, kb, 4:4 + TQ], start=(kb == 0),
                                stop=(kb == NDB - 1)),
                       [K("w_xp", kb % 2), K("xbig", kb)], [pkey],
                       f"xp{q}.{c0}.{kb}")
                if kb + 2 < NDB:
                    xpw_dma(kb + 2)
            for (c0, nco, p0, bank, pkey, dkey, dst) in groups:
                A(lambda s, p0=p0, nco=nco, bank=bank, dst=dst: s.activation(
                    out=dst[:, :], in_=bank[p0:p0 + nco, :], func=AF.Copy),
                  [pkey], [dkey], f"xpev{q}.{c0}")
            for (srw, skey, dbuf, dname) in ((brow, K("brow"), bbuf, "bbuf"),
                                             (crow, K("crow"), cbuf, "cbuf")):
                for n in range(NST):
                    bank = pb[n % 2]
                    pkey = K(f"pb{n % 2}")
                    TE(lambda t, n=n, bank=bank, srw=srw: t.matmul(
                        bank[:], esel[:, n * 128:(n + 1) * 128], srw[:, :],
                        start=True, stop=True),
                       [K("esel"), skey], [pkey], f"bc{dname}{q}.{n}")
                    A(lambda s, n=n, bank=bank, dbuf=dbuf: s.activation(
                        out=dbuf[:, n, :], in_=bank[:], func=AF.Copy),
                      [pkey], [K(dname, n)], f"bcv{dname}{q}.{n}")
            emit_dt(q, 0)
            if q == 0:
                tap("dlow", xrow[:, :], [K("xrow")])
                tap("bbuf", bbuf[:].rearrange("p a b -> p (a b)"),
                    [K("bbuf", n) for n in range(NST)])

        def emit_dt(q, db):
            bank = pb[db % 2]
            pkey = K(f"pb{db % 2}")
            sl = db % 2
            dma("gpsimd", w_dt[:, sl, :],
                h_dtw.rearrange("r (jb c) -> r jb c", c=128)[:, db, :],
                (), [K("w_dt", sl)], f"dtw{q}.{db}")
            TE(lambda t, sl=sl, bank=bank: t.matmul(
                bank[:], w_dt[:, sl, :], xrow[:, :], start=True, stop=True),
               [K("w_dt", sl), K("xrow")], [pkey], f"dt{q}.{db}")
            A(lambda s, db=db, bank=bank: s.activation(
                out=scr[:, 0:TQ], in_=bank[:], func=AF.Relu,
                bias=dtbv[:, db:db + 1]),
              [pkey, K("dtbv", db)], [K("scr")], f"dtr{q}.{db}")
            A(lambda s, db=db, bank=bank: s.activation(
                out=scr[:, TQ:2 * TQ], in_=bank[:], func=AF.Abs,
                bias=dtbv[:, db:db + 1]),
              [pkey, K("dtbv", db)], [K("scr")], f"dta{q}.{db}")
            A(lambda s: s.activation(out=scr[:, TQ:2 * TQ],
                                     in_=scr[:, TQ:2 * TQ],
                                     func=AF.Exp, scale=-1.0),
              [K("scr")], [K("scr")], f"dte{q}.{db}")
            A(lambda s: s.activation(out=scr[:, TQ:2 * TQ],
                                     in_=scr[:, TQ:2 * TQ],
                                     func=AF.Ln, bias=cst[:, 1:2]),
              [K("scr"), K("cst")], [K("scr")], f"dtl{q}.{db}")
            V(lambda v, sl=sl: v.tensor_tensor(
                out=dlt[:, sl, :], in0=scr[:, 0:TQ], in1=scr[:, TQ:2 * TQ],
                op=ALU.add),
              [K("scr")], [K("dlt", sl)], f"dtd{q}.{db}")

        def emit_scan_db(q, db):
            sl = db % 2
            if db + 1 < NDB:
                emit_dt(q, db + 1)
            V(lambda v, db=db, sl=sl: v.tensor_tensor(
                out=du[:], in0=dlt[:, sl, :], in1=xi[:, db, 4:4 + TQ],
                op=ALU.mult),
              [K("dlt", sl), K("xbig", db)], [K("du")], f"du{q}.{db}")
            V(lambda v, db=db: v.tensor_scalar(
                out=sdu[:], in0=xi[:, db, 4:4 + TQ], scalar1=ddv[:, db:db + 1],
                scalar2=None, op0=ALU.mult),
              [K("xbig", db), K("ddv", db)], [K("sdu")], f"yd{q}.{db}")
            for g in range(4):
                r = (db * 4 + g) % 2
                kdA = K("dAb", r)
                kdB = K("dBu", r)
                kst = K("stt", r)
                for i in range(4):
                    n = g * 4 + i
                    A(lambda s, db=db, r=r, i=i, n=n, sl=sl: s.activation(
                        out=dAb[:, r, i, 1:1 + TQ], in_=dlt[:, sl, :],
                        func=AF.Exp, scale=a_co[:, db, n:n + 1]),
                      [K("dlt", sl), K("a_co")], [kdA], f"dae{q}.{db}.{n}")
                V(lambda v, r=r: v.memset(dAb[:, r, :, 0:1], 0.0),
                  [kdA], [kdA], f"daz{q}.{db}.{g}")
                V(lambda v, db=db, r=r, g=g: v.tensor_copy(
                    dBu[:, r, :, 0:1],
                    carry[:, db, g * 4:(g + 1) * 4].unsqueeze(2)),
                  [K("carry", db)], [kdB], f"cin{q}.{db}.{g}")
                V(lambda v, r=r, g=g: v.tensor_tensor(
                    out=dBu[:, r, :, 1:1 + TQ],
                    in0=du[:].unsqueeze(1).broadcast_to([128, 4, TQ]),
                    in1=bbuf[:, g * 4:(g + 1) * 4, :], op=ALU.mult),
                  [K("du")] + [K("bbuf", g * 4 + i) for i in range(4)],
                  [kdB], f"dbu{q}.{db}.{g}")
                V(lambda v, r=r: v.tensor_tensor_scan(
                    stt[:, r].rearrange("p a b -> p (a b)"),
                    dAb[:, r].rearrange("p a b -> p (a b)"),
                    dBu[:, r].rearrange("p a b -> p (a b)"),
                    0.0, ALU.mult, ALU.add),
                  [kdA, kdB], [kst], f"tts{q}.{db}.{g}")
                V(lambda v, db=db, r=r, g=g: v.tensor_copy(
                    carry[:, db, g * 4:(g + 1) * 4].unsqueeze(2),
                    stt[:, r, :, TQ:TQ + 1]),
                  [kst], [K("carry", db)], f"csv{q}.{db}.{g}")
                V(lambda v, r=r, g=g: v.tensor_tensor(
                    out=stt[:, r, :, 1:1 + TQ], in0=stt[:, r, :, 1:1 + TQ],
                    in1=cbuf[:, g * 4:(g + 1) * 4, :], op=ALU.mult),
                  [kst] + [K("cbuf", g * 4 + i) for i in range(4)],
                  [kst], f"zc{q}.{db}.{g}")
                V(lambda v, r=r: v.tensor_tensor(
                    out=stt[:, r, 0:2, 1:1 + TQ], in0=stt[:, r, 0:2, 1:1 + TQ],
                    in1=stt[:, r, 2:4, 1:1 + TQ], op=ALU.add),
                  [kst], [kst], f"tr2.{q}.{db}.{g}")
                V(lambda v, r=r: v.tensor_tensor(
                    out=stt[:, r, 0:1, 1:1 + TQ], in0=stt[:, r, 0:1, 1:1 + TQ],
                    in1=stt[:, r, 1:2, 1:1 + TQ], op=ALU.add),
                  [kst], [kst], f"tr1.{q}.{db}.{g}")
                V(lambda v, r=r: v.tensor_tensor(
                    out=sdu[:], in0=sdu[:], in1=stt[:, r, 0, 1:1 + TQ],
                    op=ALU.add),
                  [K("sdu"), kst], [K("sdu")], f"ya{q}.{db}.{g}")
            V(lambda v, db=db: v.tensor_copy(xi[:, db, 4:4 + TQ], sdu[:]),
              [K("sdu"), K("xbig", db)], [K("xbig", db)], f"yw{q}.{db}")
            if q == 0 and db == NDB - 1:
                tap("yall", xbig[:], [K("xbig", d) for d in range(NDB)])

        def emit_boundary(q):
            for rdb in range(NDB):
                c = rdb // 2
                slot = c % 2
                if rdb % 2 == 0:
                    win_dma(q, c, 1, "wre")
                bank = pb[rdb % 2]
                pkey = K(f"pb{rdb % 2}")
                for kb in range(NMB):
                    TE(lambda t, slot=slot, rdb=rdb, kb=kb, bank=bank:
                       t.matmul(bank[:],
                                w_in[:, slot, kb,
                                     (rdb % 2) * 128:(rdb % 2 + 1) * 128],
                                ncma[:, kb, :], start=(kb == 0),
                                stop=(kb == NMB - 1)),
                       [K("ntw", 2 * slot), K("ntw", 2 * slot + 1),
                        K("ncma")], [pkey], f"re{q}.{rdb}.{kb}")
                A(lambda s, rdb=rdb, bank=bank: s.activation(
                    out=rsl[:, rdb % 2, :], in_=bank[:], func=AF.Silu),
                  [pkey], [K("rsl", rdb % 2)], f"rev{q}.{rdb}")
                V(lambda v, rdb=rdb: v.tensor_tensor(
                    out=xi[:, rdb, 4:4 + TQ], in0=xi[:, rdb, 4:4 + TQ],
                    in1=rsl[:, rdb % 2, :], op=ALU.mult),
                  [K("rsl", rdb % 2), K("xbig", rdb)], [K("xbig", rdb)],
                  f"y2{q}.{rdb}")
            if q == 0:
                tap("y2all", xbig[:], [K("xbig", d) for d in range(NDB)])
            op_psum = {(0, 0): (pbig0[:, 0:512], K("pbig0")),
                       (0, 1): (pbig0[:, 512:1024], K("pbig0")),
                       (1, 0): (pbig1[:, 0:512], K("pbig1")),
                       (1, 1): (pbig1[:, 512:1024], K("pbig1")),
                       (2, 0): (pb[0][:, 0:512], K("pb0")),
                       (2, 1): (pb[1][:, 0:512], K("pb1")),
                       (3, 0): (pb[2][:, 0:512], K("pb2")),
                       (3, 1): (pb[3][:, 0:512], K("pb3"))}

            def wo_dma(kb):
                s_ = kb % 4
                dma("gpsimd", w_out[:, s_, :],
                    h_wout[kb * 128:(kb + 1) * 128, :],
                    (), [K("ncmb")],
                    f"wo{q}.{kb}")

            for kb in range(4):
                wo_dma(kb)
            for kb in range(NDB):
                s_ = kb % 4
                for tb in range(NTB):
                    for dmh in range(2):
                        dst, pkey = op_psum[(tb, dmh)]
                        TE(lambda t, kb=kb, tb=tb, dmh=dmh, dst=dst, s_=s_:
                           t.matmul(
                               dst, xi[:, kb, 4 + tb * 128:4 + (tb + 1) * 128],
                               w_out[:, s_, dmh * 512:(dmh + 1) * 512],
                               start=(kb == 0), stop=False),
                           [K("xbig", kb), K("ncmb")], [pkey],
                           f"op{q}.{kb}.{tb}.{dmh}")
                if kb + 4 < NDB:
                    wo_dma(kb + 4)
            for tb in range(NTB):
                for dmh in range(2):
                    sl = (tb * 2 + dmh) % 2
                    V(lambda v, tb=tb, dmh=dmh, sl=sl: v.tensor_scalar(
                        out=rsl[:, sl, :],
                        in0=x0[:, tb, dmh * 512:(dmh + 1) * 512],
                        scalar1=rows[:, tb:tb + 1],
                        scalar2=rows[:, 4 + tb:5 + tb],
                        op0=ALU.mult, op1=ALU.add),
                      [K("x0", tb), K("rows", tb), K("rows", 4 + tb)],
                      [K("rsl", sl)], f"x1r{q}.{tb}.{dmh}")
                    dst, pkey = op_psum[(tb, dmh)]
                    TE(lambda t, dst=dst, sl=sl: t.matmul(
                        dst, iden[:], rsl[:, sl, :],
                        start=False, stop=True),
                       [K("iden"), K("rsl", sl)], [pkey],
                       f"opx1.{q}.{tb}.{dmh}")
            aokeys = [K("xbig", db) for db in range(NDB)]
            for tb in range(NTB):
                for dmh in range(2):
                    src, pkey = op_psum[(tb, dmh)]
                    A(lambda s, tb=tb, dmh=dmh, src=src: s.activation(
                        out=ao_f[:, tb, dmh * 512:(dmh + 1) * 512], in_=src,
                        func=AF.Copy),
                      [pkey] + (aokeys if (tb, dmh) == (0, 0) else
                                [K("xbig", d) for d in ao_dbs(tb)]),
                      [K("xbig", d) for d in ao_dbs(tb)],
                      f"aoev{q}.{tb}.{dmh}")
            if q == 0:
                tap("ao", ao_f.rearrange("p a b -> p (a b)"),
                    [K("xbig", d) for d in range(NDB)])

            ln_stats(lambda tb: ao_f[:, tb, :],
                     lambda tb: [K("xbig", d) for d in ao_dbs(tb)],
                     28, f"ln3.{q}")
            for tb in range(NTB):
                V(lambda v, tb=tb: v.tensor_scalar(
                    out=scr[:, 0:D], in0=ao_f[:, tb, :],
                    scalar1=rows[:, 28 + tb:29 + tb],
                    scalar2=rows[:, 32 + tb:33 + tb],
                    op0=ALU.mult, op1=ALU.add),
                  [K("xbig", d) for d in ao_dbs(tb)] +
                  [K("rows", 28 + tb), K("rows", 32 + tb)],
                  [K("scr")], f"ln3a{q}.{tb}")
                V(lambda v, tb=tb: v.tensor_tensor(
                    out=x2[:, tb, :], in0=scr[:, 0:D],
                    in1=x0[:, tb, :], op=ALU.add),
                  [K("scr"), K("x0", tb)], [K("x2", tb)], f"ln3b{q}.{tb}")
            ln_stats(lambda tb: x2[:, tb, :],
                     lambda tb: [K("x2", tb)], 48, f"ln2.{q}")
            for tb in range(NTB):
                V(lambda v, tb=tb: v.tensor_scalar(
                    out=n_tok[:, tb, :], in0=x2[:, tb, :],
                    scalar1=rows[:, 48 + tb:49 + tb],
                    scalar2=rows[:, 52 + tb:53 + tb],
                    op0=ALU.mult, op1=ALU.add),
                  [K("x2", tb), K("rows", 48 + tb), K("rows", 52 + tb)],
                  [K("ntw", tb)], f"ln2a{q}.{tb}")
            if q == 0:
                tap("x2", x2[:].rearrange("p a b -> p (a b)"),
                    [K("x2", tb) for tb in range(NTB)])
            tx_transpose(n_tok, lambda tb: [K("ntw", tb)], ncmb[:], "ncmb",
                         f"tx2.{q}")
        def emit_gate(q):
            for tb in range(NTB):
                for kb in range(NMB):
                    TE(lambda t, tb=tb, kb=kb: t.matmul(
                        pb[2][:, 0:NE], ncmb[:, kb, tb * 128:(tb + 1) * 128],
                        w_g[:, kb, :], start=(kb == 0), stop=(kb == NMB - 1)),
                       [K("ncmb"), K("w_g", kb)], [K("pb2")],
                       f"g{q}.{tb}.{kb}")
                A(lambda s, tb=tb: s.activation(
                    out=grow[:, tb, :], in_=pb[2][:, 0:NE], func=AF.Copy),
                  [K("pb2")], [K("grow", tb)], f"gev{q}.{tb}")
            rg = 88
            for tb in range(NTB):
                V(lambda v, tb=tb: v.reduce_max(
                    out=rows[:, rg + tb:rg + tb + 1], in_=grow[:, tb, :],
                    axis=AX.X),
                  [K("grow", tb)], [K("rows", rg + tb)], f"gm{q}.{tb}")
                V(lambda v, tb=tb: v.tensor_scalar(
                    out=rows[:, rg + 4 + tb:rg + 5 + tb],
                    in0=rows[:, rg + tb:rg + tb + 1], scalar1=-1.0,
                    scalar2=None, op0=ALU.mult),
                  [K("rows", rg + tb)], [K("rows", rg + 4 + tb)],
                  f"gn{q}.{tb}")
                A(lambda s, tb=tb: s.activation(
                    out=et[:, tb, :], in_=grow[:, tb, :], func=AF.Exp,
                    bias=rows[:, rg + 4 + tb:rg + 5 + tb]),
                  [K("grow", tb), K("rows", rg + 4 + tb)], [K("et", tb)],
                  f"ge{q}.{tb}")
                V(lambda v, tb=tb: v.reduce_max(
                    out=rows[:, rg + 8 + tb:rg + 9 + tb], in_=et[:, tb, :],
                    axis=AX.X),
                  [K("et", tb)], [K("rows", rg + 8 + tb)], f"gm1{q}.{tb}")
                V(lambda v, tb=tb: v.tensor_scalar(
                    out=gmask[:, tb, :], in0=et[:, tb, :],
                    scalar1=rows[:, rg + 8 + tb:rg + 9 + tb], scalar2=None,
                    op0=ALU.is_equal),
                  [K("et", tb), K("rows", rg + 8 + tb)], [K("gmask", tb)],
                  f"gq1{q}.{tb}")
                V(lambda v, tb=tb: v.tensor_scalar(
                    out=grow[:, tb, :], in0=gmask[:, tb, :], scalar1=-1.0,
                    scalar2=1.0, op0=ALU.mult, op1=ALU.add),
                  [K("gmask", tb)], [K("grow", tb)], f"ga{q}.{tb}")
                V(lambda v, tb=tb: v.tensor_tensor(
                    out=grow[:, tb, :], in0=grow[:, tb, :], in1=et[:, tb, :],
                    op=ALU.mult),
                  [K("grow", tb), K("et", tb)], [K("grow", tb)],
                  f"ge2{q}.{tb}")
                V(lambda v, tb=tb: v.reduce_max(
                    out=rows[:, rg + 12 + tb:rg + 13 + tb],
                    in_=grow[:, tb, :], axis=AX.X),
                  [K("grow", tb)], [K("rows", rg + 12 + tb)], f"gm2{q}.{tb}")
                V(lambda v, tb=tb: v.tensor_tensor(
                    out=rows[:, rg + 16 + tb:rg + 17 + tb],
                    in0=rows[:, rg + 8 + tb:rg + 9 + tb],
                    in1=rows[:, rg + 12 + tb:rg + 13 + tb], op=ALU.add),
                  [K("rows", rg + 8 + tb), K("rows", rg + 12 + tb)],
                  [K("rows", rg + 16 + tb)], f"gs{q}.{tb}")
                V(lambda v, tb=tb: v.reciprocal(
                    out=rows[:, rg + 16 + tb:rg + 17 + tb],
                    in_=rows[:, rg + 16 + tb:rg + 17 + tb]),
                  [K("rows", rg + 16 + tb)], [K("rows", rg + 16 + tb)],
                  f"gr{q}.{tb}")
                V(lambda v, tb=tb: v.tensor_tensor(
                    out=rows[:, rg + 20 + tb:rg + 21 + tb],
                    in0=rows[:, rg + 8 + tb:rg + 9 + tb],
                    in1=rows[:, rg + 16 + tb:rg + 17 + tb], op=ALU.mult),
                  [K("rows", rg + 8 + tb), K("rows", rg + 16 + tb)],
                  [K("rows", rg + 20 + tb)], f"gw1{q}.{tb}")
                V(lambda v, tb=tb: v.tensor_scalar(
                    out=cw[:, tb, :], in0=grow[:, tb, :],
                    scalar1=rows[:, rg + 12 + tb:rg + 13 + tb], scalar2=None,
                    op0=ALU.is_equal),
                  [K("grow", tb), K("rows", rg + 12 + tb)], [K("cw", tb)],
                  f"gq2{q}.{tb}")
                V(lambda v, tb=tb: v.tensor_tensor(
                    out=rows[:, rg + 12 + tb:rg + 13 + tb],
                    in0=rows[:, rg + 12 + tb:rg + 13 + tb],
                    in1=rows[:, rg + 16 + tb:rg + 17 + tb], op=ALU.mult),
                  [K("rows", rg + 12 + tb), K("rows", rg + 16 + tb)],
                  [K("rows", rg + 12 + tb)], f"gw2{q}.{tb}")
                V(lambda v, tb=tb: v.tensor_scalar(
                    out=cw[:, tb, :], in0=cw[:, tb, :],
                    scalar1=rows[:, rg + 12 + tb:rg + 13 + tb], scalar2=None,
                    op0=ALU.mult),
                  [K("cw", tb), K("rows", rg + 12 + tb)], [K("cw", tb)],
                  f"gc2{q}.{tb}")
                V(lambda v, tb=tb: v.scalar_tensor_tensor(
                    out=cw[:, tb, :], in0=gmask[:, tb, :],
                    scalar=rows[:, rg + 20 + tb:rg + 21 + tb],
                    in1=cw[:, tb, :], op0=ALU.mult, op1=ALU.add),
                  [K("gmask", tb), K("rows", rg + 20 + tb), K("cw", tb)],
                  [K("cw", tb)], f"gc1{q}.{tb}")
            if q == 0:
                tap("cw", cw[:].rearrange("p a b -> p (a b)"),
                    [K("cw", tb) for tb in range(NTB)])

        def emit_expert_mu(q, e):
            wsrc = (h_wu[e] if e < NE else h_w1).rearrange(
                "(kb p) f -> p kb f", p=128)
            for c in range(8):
                slot = c % 2
                dma("gpsimd", w_u[:, slot, :, :],
                    wsrc[:, :, c * 256:(c + 1) * 256],
                    (), [K("w_u", slot)], f"wu{q}.{e}.{c}")
                for j in range(2):
                    jb = c * 2 + j
                    bank = pb[2 + (jb % 2)]
                    pkey = K(f"pb{2 + (jb % 2)}")
                    for kb in range(NMB):
                        TE(lambda t, slot=slot, j=j, kb=kb, bank=bank:
                           t.matmul(bank[:],
                                    w_u[:, slot, kb, j * 128:(j + 1) * 128],
                                    ncmb[:, kb, :], start=(kb == 0),
                                    stop=(kb == NMB - 1)),
                           [K("w_u", slot), K("ncmb")], [pkey],
                           f"mu{q}.{e}.{jb}.{kb}")
                    A(lambda s, jb=jb, bank=bank: s.activation(
                        out=h2[:, jb, :], in_=bank[:], func=AF.Gelu),
                      [pkey], [K("h2")], f"mug{q}.{e}.{jb}")
        def emit_expert_md(q, e):
            wdsrc = (h_wd[e] if e < NE else h_w2).rearrange(
                "(kb p) d -> p kb d", p=128)

            def wd_dma(f):
                dma("gpsimd", w_d[:, f % 2, :],
                    wdsrc[:, f, :],
                    (), [K("w_d", f % 2)], f"wd{q}.{e}.{f}")

            wd_dma(0)
            wd_dma(1)
            md_psum = {(0, 0): (pbig0[:, 0:512], K("pbig0")),
                       (0, 1): (pbig0[:, 512:1024], K("pbig0")),
                       (1, 0): (pbig1[:, 0:512], K("pbig1")),
                       (1, 1): (pbig1[:, 512:1024], K("pbig1")),
                       (2, 0): (pb[0][:, 0:512], K("pb0")),
                       (2, 1): (pb[1][:, 0:512], K("pb1")),
                       (3, 0): (pb[2][:, 0:512], K("pb2")),
                       (3, 1): (pb[3][:, 0:512], K("pb3"))}
            for kb in range(NFB):
                for tb in range(NTB):
                    for dmh in range(2):
                        dst, pkey = md_psum[(tb, dmh)]
                        TE(lambda t, kb=kb, tb=tb, dmh=dmh, dst=dst: t.matmul(
                            dst, h2[:, kb, tb * 128:(tb + 1) * 128],
                            w_d[:, kb % 2, dmh * 512:(dmh + 1) * 512],
                            start=(kb == 0), stop=(kb == NFB - 1)),
                           [K("h2"), K("w_d", kb % 2)], [pkey],
                           f"md{q}.{e}.{kb}.{tb}.{dmh}")
                if kb + 2 < NFB:
                    wd_dma(kb + 2)
            for (tb, dmh) in ((3, 0), (3, 1), (2, 0), (2, 1),
                              (0, 0), (0, 1), (1, 0), (1, 1)):
                src_, pkey = md_psum[(tb, dmh)]
                emit_acc_ev(q, e, tb, dmh, src_, pkey)

        def emit_acc_ev(q, e, tb, dmh, src_, pkey):
            dsl = slice(dmh * 512, (dmh + 1) * 512)
            if e == 0:
                V(lambda v: v.tensor_scalar(
                    out=acc[:, tb, dsl], in0=src_, scalar1=cw[:, tb, e:e + 1],
                    scalar2=None, op0=ALU.mult),
                  [pkey, K("cw", tb)], [K("acc", tb)], f"acv{q}.{e}.{tb}.{dmh}")
            elif e < NE:
                V(lambda v: v.scalar_tensor_tensor(
                    out=acc[:, tb, dsl], in0=src_, scalar=cw[:, tb, e:e + 1],
                    in1=acc[:, tb, dsl], op0=ALU.mult, op1=ALU.add),
                  [pkey, K("cw", tb), K("acc", tb)], [K("acc", tb)],
                  f"acv{q}.{e}.{tb}.{dmh}")
            else:
                V(lambda v: v.tensor_tensor(
                    out=acc[:, tb, dsl], in0=src_, in1=acc[:, tb, dsl],
                    op=ALU.add),
                  [pkey, K("acc", tb)], [K("acc", tb)], f"acv{q}.8.{tb}.{dmh}")

        def emit_ln4(q):
            q0 = q * TQ
            ln_stats(lambda tb: acc[:, tb, :],
                     lambda tb: [K("acc", tb)], 68, f"ln4.{q}")
            for tb in range(NTB):
                V(lambda v, tb=tb: v.tensor_scalar(
                    out=scr[:, 0:D], in0=acc[:, tb, :],
                    scalar1=rows[:, 68 + tb:69 + tb],
                    scalar2=rows[:, 72 + tb:73 + tb],
                    op0=ALU.mult, op1=ALU.add),
                  [K("acc", tb), K("rows", 68 + tb), K("rows", 72 + tb)],
                  [K("scr")], f"ln4a{q}.{tb}")
                V(lambda v, tb=tb: v.tensor_tensor(
                    out=acc[:, tb, :], in0=scr[:, 0:D],
                    in1=x2[:, tb, :], op=ALU.add),
                  [K("scr"), K("x2", tb), K("acc", tb)], [K("acc", tb)],
                  f"ln4b{q}.{tb}")
                dma("sync", h_out[q0 + tb * 128:q0 + (tb + 1) * 128, :],
                    acc[:, tb, :],
                    [K("acc", tb)], [K("hbm_out", (q, tb))],
                    f"out{q}.{tb}")

        def mamba_chunks(q):
            ch = [lambda q=q: emit_pre(q)]
            for part in range(4):
                ch.append(lambda q=q, part=part: emit_inproj(q, part))
            ch.append(lambda q=q: emit_xproj(q))
            for db in range(NDB):
                ch.append(lambda q=q, db=db: emit_scan_db(q, db))
            return ch

        def moe_chunks(q):
            ch = []
            for e in range(NE + 1):
                ch.append(lambda q=q, e=e: emit_expert_mu(q, e))
                ch.append(lambda q=q, e=e: emit_expert_md(q, e))
            ch.append(lambda q=q: emit_ln4(q))
            return ch

        for ch in mamba_chunks(0):
            ch()
        for q in range(NQ):
            emit_boundary(q)
            mc = moe_chunks(q)
            mc.insert(1, lambda q=q: emit_gate(q))
            sc = mamba_chunks(q + 1) if q + 1 < NQ else []
            wts = [0.35] * min(6, len(sc)) + [1.0] * max(0, len(sc) - 6)
            totw = sum(wts)
            si = 0
            accw = 0.0
            for i, m in enumerate(mc):
                tgtw = (i + 1) * totw / len(mc)
                while si < len(sc) and accw < tgtw - 1e-9:
                    sc[si]()
                    accw += wts[si]
                    si += 1
                m()
            while si < len(sc):
                sc[si]()
                si += 1

        S.add("sync", lambda g: None,
              reads=[K("hbm_out", (q, tb)) for q in range(NQ)
                     for tb in range(NTB)],
              writes=(), tag="final_fence")

        S.finalize(block, sems, dma_sems)
    return nc


_NC_CACHE = {}


def _get_program():
    if "nc" not in _NC_CACHE:
        _NC_CACHE["nc"] = build_program()
    return _NC_CACHE["nc"]


def kernel(L0_attn_x, L0_attn_in_proj_w, L0_attn_conv_w, L0_attn_conv_b,
           L0_attn_x_proj_w, L0_attn_dt_proj_w, L0_attn_dt_proj_b,
           L0_attn_A_log, L0_attn_D, L0_attn_out_proj_w, L0_attn_norm_w,
           L0_ffn_Wg, L0_ffn_Wu, L0_ffn_Wd, L0_ffn_W1, L0_ffn_W2):
    import ml_dtypes
    x = np.ascontiguousarray(np.asarray(L0_attn_x, dtype=np.float32))
    iden_c = np.eye(128, dtype=ml_dtypes.bfloat16)
    esel_c = np.zeros((NST, NST * 128), dtype=ml_dtypes.bfloat16)
    for n in range(NST):
        esel_c[n, n * 128:(n + 1) * 128] = 1.0

    def f32(a):
        return np.ascontiguousarray(np.asarray(a, dtype=np.float32))

    shared = {
        "w_in": f32(L0_attn_in_proj_w),
        "conv_w": f32(L0_attn_conv_w),
        "conv_b": f32(L0_attn_conv_b),
        "x_proj_w": f32(L0_attn_x_proj_w),
        "dt_w": f32(L0_attn_dt_proj_w),
        "dt_b": f32(L0_attn_dt_proj_b),
        "a_log": f32(L0_attn_A_log),
        "d_vec": f32(L0_attn_D),
        "w_out": f32(L0_attn_out_proj_w),
        "norm_w": f32(L0_attn_norm_w),
        "wg": f32(L0_ffn_Wg),
        "wu": f32(L0_ffn_Wu),
        "wd": f32(L0_ffn_Wd),
        "w1": f32(L0_ffn_W1),
        "w2": f32(L0_ffn_W2),
        "c_iden": iden_c,
        "c_esel": esel_c,
    }
    nc = _get_program()
    in_maps = []
    for b in range(B):
        m = dict(shared)
        m["x"] = np.ascontiguousarray(x[b])
        in_maps.append(m)
    res = run_bass_kernel_spmd(nc, in_maps, core_ids=list(range(B)))
    out = np.stack([np.asarray(res.results[b]["out"]) for b in range(B)],
                   axis=0)
    if DBG is not None:
        return out.astype(np.float32), np.asarray(res.results[0]["dbg"])
    return out.astype(np.float32)
